# version 22
# baseline (speedup 1.0000x reference)
import numpy as np
import ml_dtypes
from contextlib import ExitStack

import concourse.bass as bass
import concourse.mybir as mybir
from concourse.bass_utils import run_bass_kernel_spmd

F32 = mybir.dt.float32
BF16 = mybir.dt.bfloat16
AF = mybir.ActivationFunctionType
ALU = mybir.AluOpType
AX = mybir.AxisListType
BB = ml_dtypes.bfloat16

E = 1024
H = 16
CH = 128


class Cfg:
    def __init__(self, T=2048, PCH=112, ncores=8):
        self.T = T
        self.PCH = PCH
        self.NTILE = T // 512
        self.NCHUNK = T // CH
        self.ncores = ncores


class Sems:
    def __init__(self, nc, es):
        self.nc, self.es = nc, es
        self.h, self.n = {}, {}

    def _get(self, name):
        if name not in self.h:
            self.h[name] = self.es.enter_context(self.nc.semaphore(name))
            self.n[name] = 0
        return self.h[name]

    def inc(self, inst, name, k=1):
        inst.then_inc(self._get(name), k)
        self.n[name] += k
        return self.n[name]

    def wait(self, eng, name, val=None):
        eng.wait_ge(self._get(name), self.n[name] if val is None else val)

    def val(self, name):
        self._get(name)
        return self.n[name]


class _PhaseCutExc(Exception):
    pass


def build_nc(cfg: Cfg, phases=4, p3sub=9, pskip=()):
    _PhaseCut = _PhaseCutExc
    nc = bass.Bass("TRN2", target_bir_lowering=False)
    T, PCH, NTILE, NCHUNK = cfg.T, cfg.PCH, cfg.NTILE, cfg.NCHUNK
    NPT = PCH // 4

    def din(name, shape, dtype):
        return nc.dram_tensor(name, shape, dtype, kind="ExternalInput").ap()

    d_xpfx = din("xpfx", [E, PCH * CH], F32)
    d_xsl = din("xsl", [E, T], F32)
    d_xm1 = din("xm1", [128, 8], F32)
    d_pmask = din("pmask", [128, PCH], F32)
    d_m0 = din("m0", [128, 1], F32)
    d_qkvw = din("qkv_w", [E, 3 * E], F32)
    d_outw = din("out_w", [E, E], F32)
    d_wrt = din("wrT", [E, E], F32)
    d_wkt = din("wkT", [E, E], F32)
    d_wvt = din("wvT", [E, E], F32)
    d_brow = din("brow", [1, 3 * E], F32)
    d_outb = din("outb_fm", [128, 8], F32)
    d_mur = din("mur_fm", [128, 8], F32)
    d_muk = din("muk_fm", [128, 8], F32)
    d_ident = din("ident", [128, 128], BF16)
    d_mask = din("maskc", [128, 4 * CH], BF16)
    d_e31 = din("e31", [128, 31], BF16)
    d_ebc = din("ebc", [16, 8 * 128], BF16)
    d_ones = din("ones", [128, 512], BF16)
    d_y = nc.dram_tensor("y", [8, 128, T], F32, kind="ExternalOutput").ap()
    d_x1 = nc.dram_tensor("x1d", [8, 128, T], F32).ap()

    es = ExitStack()
    try:
      with es:
        S = Sems(nc, es)
        blk = es.enter_context(nc.Block())

        def mksb(stack, name, shape, dtype):
            return stack.enter_context(nc.sbuf_tensor(name, shape, dtype))

        c_id = mksb(es, "c_id", [128, 128], BF16)
        c_mask = mksb(es, "c_mask", [128, 4 * CH], BF16)
        c_e31 = mksb(es, "c_e31", [128, 31], BF16)
        c_ebc = mksb(es, "c_ebc", [16, 8 * 128], BF16)
        c_ones = mksb(es, "c_ones", [128, 512], BF16)
        c_brow = mksb(es, "c_brow", [1, E], BF16)
        c_outb = mksb(es, "c_outb", [128, 8], F32)
        c_mur = mksb(es, "c_mur", [128, 8], F32)
        c_muk = mksb(es, "c_muk", [128, 8], F32)
        c_pm = mksb(es, "c_pm", [128, PCH], F32)
        c_m0 = mksb(es, "c_m0", [128, 1], F32)
        x1m1 = mksb(es, "x1m1", [128, 8], F32)
        xlast = mksb(es, "xlast", [128, 8 * NTILE], F32)

        es13 = ExitStack()
        pkm = mksb(es13, "pkm", [128, 128], BF16)
        xb = [mksb(es13, "xb0", [128, 8 * 512], BF16),
              mksb(es13, "xb1", [128, 8 * 512], BF16)]
        xstg = [mksb(es13, "xg0", [128, 512], F32),
                mksb(es13, "xg1", [128, 512], F32)]
        kexp = mksb(es13, "kexp", [128, 16 * 128], BF16)
        sexp = mksb(es13, "sexp", [128, 16 * 64], BF16)
        kfm = mksb(es13, "kfm", [128, 8 * T], BF16)
        vaug = mksb(es13, "vaug", [128, NCHUNK * H * 66], BF16)
        snaps = mksb(es13, "snaps", [128, NCHUNK * 528], BF16)
        sabs = mksb(es13, "sabs", [128, 8 * 65], F32)
        w_q = mksb(es13, "w_q", [128, 8 * 1024], BF16)
        w_out = mksb(es13, "w_out", [128, 8 * 1024], BF16)

        @blk.sync
        def _(sync):
            for src, dst in [(d_ident, c_id), (d_mask, c_mask), (d_e31, c_e31),
                             (d_ebc, c_ebc), (d_ones, c_ones), (d_outb, c_outb),
                             (d_mur, c_mur), (d_muk, c_muk), (d_pmask, c_pm),
                             (d_m0, c_m0)]:
                S.inc(sync.dma_start(out=dst[:, :], in_=src), "cl", 16)

        es1 = ExitStack()
        w_kv = mksb(es1, "w_kv", [128, 8 * 2048], BF16)
        c_brkv = mksb(es1, "c_brkv", [1, 2 * E], BF16)
        ktm = [mksb(es1, "ktm0", [128, E], BF16),
               mksb(es1, "ktm1", [128, E], BF16)]
        vtmp = [mksb(es1, "vtm0", [128, H * 66], BF16),
                mksb(es1, "vtm1", [128, H * 66], BF16)]
        _r1b = mksb(es1, "r1b0", [128, E], BF16)
        r1b = [_r1b, _r1b]
        _n1f = mksb(es1, "n1f0", [128, E], F32)
        n1f = [_n1f, _n1f]
        _ebb = mksb(es1, "eb0", [128, E], BF16)
        ebb = [_ebb, _ebb]

        @blk.gpsimd
        def _(g):
            S.inc(g.dma_start(out=c_brow[:, :], in_=d_brow[:, 0:E]), "swb", 16)
            S.inc(g.dma_start(out=c_brkv[:, :], in_=d_brow[:, E:]), "swb", 16)
            for c in range(8):
                S.inc(g.dma_start(out=w_kv[:, c * 2048:(c + 1) * 2048],
                                  in_=d_qkvw[c * 128:(c + 1) * 128, 1024:]),
                      "swkv", 16)
            for c in range(8):
                S.inc(g.dma_start(out=w_q[:, c * 1024:(c + 1) * 1024],
                                  in_=d_qkvw[c * 128:(c + 1) * 128, 0:1024]),
                      "swq", 16)
            for c in range(8):
                S.inc(g.dma_start(out=w_out[:, c * 1024:(c + 1) * 1024],
                                  in_=d_outw[c * 128:(c + 1) * 128, :]),
                      "swo", 16)

        @blk.vector
        def _(v):
            S.wait(v, "cl")
            va = vaug[:, :].rearrange("p (j s) -> p j s", s=66)
            S.inc(v.memset(va[:, :, 64:66], 0.0), "vc", 1)
            S.wait(v, "vc")
            S.inc(v.memset(va[:, :, 64:65], 1.0), "vc", 1)
            S.inc(v.memset(pkm[:, :], 0.0), "vc", 1)
            S.inc(v.memset(sabs[:, :], 0.0), "vc", 1)
            S.inc(v.memset(vtmp[0][:, :], 0.0), "vc", 1)
            S.inc(v.memset(vtmp[1][:, :], 0.0), "vc", 1)
            S.wait(v, "vc")
            for _vt in vtmp:
                vv = _vt[:, :].rearrange("p (h s) -> p h s", s=66)
                S.inc(v.memset(vv[:, :, 64:65], 1.0), "vc", 1)

        es1p = ExitStack()
        p_kv_k = es1p.enter_context(nc.psum_tensor("pkvk", [128, 1024], F32))
        p_kv_v = es1p.enter_context(nc.psum_tensor("pkvv", [128, 1024], F32))
        p_st = [es1p.enter_context(nc.psum_tensor(f"pst{i}", [128, 512], F32))
                for i in range(2)]
        p_tr = es1p.enter_context(nc.psum_tensor("ptr", [128, 1024], BF16))

        mk = dict(eluv=[], elua=[], stT=[], xc={}, pkfree=[], cmin=0)

        def emit_xload(tile, which, key):
            par = tile % 2
            src = d_xpfx if which == "pfx" else d_xsl
            for c in range(8):
                @blk.sync
                def _(sync, c=c):
                    S.wait(sync, "xc", max(0, S.val("xc") - 1))
                    S.inc(sync.dma_start(
                        out=xstg[c % 2][:, :],
                        in_=src[c * 128:(c + 1) * 128,
                                tile * 512:(tile + 1) * 512]),
                        "xl%d" % (c % 2), 16)

                @blk.gpsimd
                def _(g_, c=c):
                    S.wait(g_, "xl%d" % (c % 2))
                    if c == 0:
                        S.wait(g_, "xbfree", S.val("xbfree"))
                    S.inc(g_.tensor_copy(xb[par][:, c * 512:(c + 1) * 512],
                                         xstg[c % 2][:, :]), "xc", 1)
            mk["xc"][key] = S.val("xc")

        def emit_kv_chunk(j, xtile_par, jj, masked_col, own_j, xc_key,
                          vbias=True):
            par = j % 2

            def xchunk(c):
                return xb[xtile_par][:, c * 512 + jj * 128:
                                     c * 512 + jj * 128 + 128]

            @blk.tensor
            def _(t):
                if jj == 0:
                    S.wait(t, "xc", mk["xc"][xc_key])
                if mk["pkfree"]:
                    S.wait(t, "eluv", mk["pkfree"][-1])
                if "proj" in pskip:
                    S.inc(t.nop(), "projK", 1)
                    S.inc(t.nop(), "projT", 2)
                    return
                for half in range(2):
                    for c in range(8):
                        t.matmul(p_kv_k[:, half * 512:(half + 1) * 512],
                                 xchunk(c),
                                 w_kv[:, c * 2048 + half * 512:
                                      c * 2048 + (half + 1) * 512],
                                 start=(c == 0), stop=False)
                    inst = t.matmul(p_kv_k[:, half * 512:(half + 1) * 512],
                             c_ones[0:1, 0:128],
                             c_brkv[:, half * 512:(half + 1) * 512],
                             start=False, stop=True)
                S.inc(inst, "projK", 1)
                if mk["eluv"]:
                    S.wait(t, "eluv", mk["eluv"][-1])
                for half in range(2):
                    for c in range(8):
                        inst = t.matmul(
                            p_kv_v[:, half * 512:(half + 1) * 512],
                            xchunk(c),
                            w_kv[:, c * 2048 + 1024 + half * 512:
                                 c * 2048 + 1024 + (half + 1) * 512],
                            start=(c == 0),
                            stop=(c == 7 and not vbias))
                    if vbias:
                        inst = t.matmul(p_kv_v[:, half * 512:(half + 1) * 512],
                                        c_ones[0:1, 0:128],
                                        c_brkv[:, 1024 + half * 512:
                                               1024 + (half + 1) * 512],
                                        start=False, stop=True)
                    S.inc(inst, "projT", 1)

            @blk.vector
            def _(v):
                S.wait(v, "projK")
                S.wait(v, "elua")
                if "elu" in pskip:
                    S.inc(v.nop(), "eluv", 2)
                    mk["cmin"] = S.val("eluv")
                    return
                S.inc(v.tensor_scalar_min(n1f[par][:, :], p_kv_k[:, :], 0.0),
                      "eluv", 1)
                mk["cmin"] = S.val("eluv")
                S.inc(v.tensor_scalar_max(r1b[par][:, :], p_kv_k[:, :], 0.0),
                      "eluv", 1)

            mk["pkfree"].append(S.val("eluv"))

            @blk.scalar
            def _(sc):
                S.wait(sc, "eluv", mk["cmin"])
                if "act" in pskip:
                    S.inc(sc.nop(), "elua", 1)
                    return
                i1 = sc.activation(ebb[par][:, :], n1f[par][:, :], AF.Exp,
                                   bias=0.0, scale=1.0)
                S.inc(i1, "elua", 1)

            @blk.vector
            def _(v):
                S.wait(v, "elua")
                S.wait(v, "eluv", mk["pkfree"][-1])
                if mk["stT"]:
                    S.wait(v, "stT", mk["stT"][-1])
                if "elu" in pskip:
                    for _i in range(4 if masked_col is not None else 3):
                        S.inc(v.nop(), "eluv", 1)
                    return
                S.inc(v.tensor_add(ktm[par][:, :], r1b[par][:, :],
                                   ebb[par][:, :]), "eluv", 1)
                if masked_col is not None:
                    S.wait(v, "eluv")
                    S.inc(v.tensor_scalar_mul(
                        ktm[par][:, :], ktm[par][:, :],
                        c_pm[:, masked_col:masked_col + 1]), "eluv", 1)
                if own_j is None:
                    vdst = vtmp[par][:, :].rearrange("p (h s) -> p h s", s=66)
                else:
                    vdst = vaug[:, own_j * H * 66:(own_j + 1) * H * 66
                                ].rearrange("p (h s) -> p h s", s=66)
                for half in range(2):
                    S.wait(v, "projT", S.val("projT") - (1 - half))
                    S.inc(v.tensor_copy(
                        vdst[:, half * 8:(half + 1) * 8, 0:64],
                        p_kv_v[:, half * 512:(half + 1) * 512].rearrange(
                            "p (h s) -> p h s", s=64)), "eluv", 1)

            mk["eluv"].append(S.val("eluv"))
            mk["elua"].append(S.val("elua"))
            eluv_end = S.val("eluv")

            def do_state():
                if own_j is not None:
                    @blk.tensor
                    def _(t):
                        S.wait(t, "eluv", eluv_end)
                        S.wait(t, "ktr", S.val("ktr"))
                        for c in range(8):
                            inst = t.transpose(
                                p_tr[:, c * 128:(c + 1) * 128],
                                ktm[par][:, c * 128:(c + 1) * 128],
                                c_id[:, :])
                        S.inc(inst, "trT", 1)

                    @blk.vector
                    def _(v):
                        S.wait(v, "trT")
                        kdst = kfm[:, :].rearrange("p (c t) -> p c t", t=T)
                        S.inc(v.tensor_copy(
                            kdst[:, :, own_j * 128:(own_j + 1) * 128],
                            p_tr[:, :].rearrange("p (c t) -> p c t", t=128)),
                            "ktr", 1)

                @blk.tensor
                def _(t):
                    S.wait(t, "eluv", eluv_end)
                    if "state" in pskip:
                        S.inc(t.nop(), "stT", 1)
                        return
                    if own_j is not None:
                        S.wait(t, "snapv", S.val("snapv"))
                    vsrc = (vtmp[par] if own_j is None else
                            vaug[:, own_j * H * 66:(own_j + 1) * H * 66])
                    for h in range(H):
                        bank = p_st[h // 8]
                        pair_col = ((h % 8) // 2) * 65
                        half = h % 2
                        if own_j is None:
                            st = (j == 0 and h in (0, 1, 8, 9))
                            sp = False
                        else:
                            st = h in (0, 1, 8, 9)
                            sp = h in (6, 7, 14, 15)
                        inst = t.matmul(
                            bank[half * 64:half * 64 + 64,
                                 pair_col:pair_col + 65],
                            ktm[par][:, h * 64:(h + 1) * 64],
                            vsrc[:, h * 66:h * 66 + 65],
                            start=st, stop=sp, skip_group_check=True)
                    S.inc(inst, "stT", 1)

                mk["stT"].append(S.val("stT"))

                if own_j is not None:
                    @blk.vector
                    def _(v):
                        S.wait(v, "stT")
                        sdst = snaps[:, own_j * 528:(own_j + 1) * 528]
                        S.inc(v.tensor_copy(
                            sdst.rearrange("p (c s) -> p c s",
                                           s=66)[:, :, 0:65],
                            sabs[:, :].rearrange("p (c s) -> p c s", s=65)),
                            "snapv", 1)
                        S.wait(v, "snapv")
                        for b in range(2):
                            inst = v.tensor_add(
                                sabs[:, b * 260:(b + 1) * 260],
                                sabs[:, b * 260:(b + 1) * 260],
                                p_st[b][:, 0:260])
                            S.inc(inst, "snapv", 1)
                        S.wait(v, "snapv")

            return do_state

        @blk.tensor
        def _(t):
            S.wait(t, "swkv")
            S.wait(t, "swb")
            S.wait(t, "cl")

        @blk.vector
        def _(v):
            S.wait(v, "cl")
            S.wait(v, "vc")

        emit_xload(0, "pfx", ("pfx", 0))
        pend = None
        for j in range(PCH):
            if j % 4 == 0 and j // 4 + 1 < NPT:
                emit_xload(j // 4 + 1, "pfx", ("pfx", j // 4 + 1))
            nxt = emit_kv_chunk(j, (j // 4) % 2, j % 4, masked_col=j,
                                own_j=None, xc_key=("pfx", j // 4),
                                vbias=False)
            if pend is not None:
                pend()
            pend = nxt
            if j % 4 == 3:
                @blk.tensor
                def _(t):
                    S.inc(t.nop(), "xbfree", 1)
        pend()

        rowT2 = mksb(es1, "rowT2", [1, H * 64], BF16)
        scc = mksb(es1, "scc", [128, 8], F32)

        @blk.vector
        def _(v):
            S.wait(v, "stT")
            for b in range(2):
                S.inc(v.tensor_copy(scc[:, b * 4:(b + 1) * 4],
                                    p_st[b][:, 64:260:65]), "vbc", 1)

        @blk.gpsimd
        def _(g):
            S.wait(g, "vbc")
            for h in range(H):
                b, half = h // 8, h % 2
                pr = (h % 8) // 2
                S.inc(g.dma_start(
                    out=rowT2[0:1, h * 64:(h + 1) * 64],
                    in_=scc[half * 64:half * 64 + 64, b * 4 + pr:
                            b * 4 + pr + 1]),
                    "vbf", 16)

        @blk.tensor
        def _(t):
            S.wait(t, "vbf")
            for h in range(H):
                b, half = h // 8, h % 2
                pair_col = ((h % 8) // 2) * 65
                inst = t.matmul(
                    p_st[b][half * 64:half * 64 + 64,
                            pair_col:pair_col + 64],
                    rowT2[0:1, h * 64:(h + 1) * 64],
                    c_brkv[0:1, 1024 + h * 64:1024 + (h + 1) * 64],
                    start=False, stop=(h in (6, 7, 14, 15)),
                    skip_group_check=True)
            S.inc(inst, "stT", 1)

        @blk.vector
        def _(v):
            S.wait(v, "stT")
            for b in range(2):
                S.inc(v.tensor_copy(sabs[:, b * 260:(b + 1) * 260],
                                    p_st[b][:, 0:260]), "snapv", 1)
            S.wait(v, "snapv")

        pend = None
        for j in range(NCHUNK):
            if j % 4 == 0:
                emit_xload(j // 4, "own", ("own", j // 4))
            nxt = emit_kv_chunk(j, (j // 4) % 2, j % 4, masked_col=None,
                                own_j=j, xc_key=("own", j // 4))
            if pend is not None:
                pend()
            pend = nxt
            if j % 4 == 3:
                @blk.tensor
                def _(t):
                    S.inc(t.nop(), "xbfree", 1)
        pend()

        @blk.tensor
        def _(t):
            S.wait(t, "eluv", S.val("eluv"))
            S.wait(t, "elua", S.val("elua"))
            S.wait(t, "snapv", S.val("snapv"))
            S.wait(t, "ktr", S.val("ktr"))
            S.inc(t.nop(), "p1done", 1)

        es1p.close()
        es1.close()

        if phases < 2:
            raise _PhaseCut
        es2 = ExitStack()
        p_a8 = es2.enter_context(nc.psum_tensor("pm1a", [128, 512], F32))
        p_b8 = es2.enter_context(nc.psum_tensor("pm1b", [128, 512], F32))
        p_n8 = es2.enter_context(nc.psum_tensor("pm1n", [16, 512], F32))

        m1_x = mksb(es2, "m1x", [128, 8], F32)
        m1_xb = mksb(es2, "m1xb", [128, 8], BF16)
        m1_q = mksb(es2, "m1q", [128, 8], BF16)
        m1_r1 = mksb(es2, "m1r1", [128, 8], BF16)
        m1_n1 = mksb(es2, "m1n1", [128, 8], F32)
        m1_e = mksb(es2, "m1e", [128, 8], BF16)
        m1_rn = mksb(es2, "m1rn", [16, 2], F32)
        m1_rnb = mksb(es2, "m1rnb", [16, 1], BF16)
        m1_rbc = mksb(es2, "m1rbc", [128, 8], BF16)
        m1_ao = mksb(es2, "m1ao", [128, 8], BF16)
        m1_a = mksb(es2, "m1a", [128, 8], F32)
        m1_ab = mksb(es2, "m1ab", [128, 8], BF16)
        m1_sq = mksb(es2, "m1sq", [128, 8], BF16)
        m1_s1 = mksb(es2, "m1s1", [128, 1], F32)
        m1_s2 = mksb(es2, "m1s2", [128, 1], F32)
        m1_t0 = mksb(es2, "m1t0", [128, 1], F32)
        m1_t1 = mksb(es2, "m1t1", [128, 1], F32)
        m1_se = mksb(es2, "m1se", [128, 16 * 64], BF16)

        @blk.sync
        def _(sync):
            S.wait(sync, "p1done")
            S.inc(sync.dma_start(out=m1_x[:, :], in_=d_xm1), "m1l", 16)

        @blk.vector
        def _(v):
            S.wait(v, "m1l")
            S.inc(v.tensor_copy(m1_xb[:, :], m1_x[:, :]), "m1v", 1)

        @blk.tensor
        def _(t):
            S.wait(t, "p1done")
            S.wait(t, "swq")
            S.wait(t, "m1v")
            for f in range(8):
                for c in range(8):
                    t.matmul(p_a8[:, f:f + 1],
                             w_q[:, c * 1024 + f * 128:
                                 c * 1024 + (f + 1) * 128],
                             m1_xb[:, c:c + 1],
                             start=(c == 0), stop=False)
                inst = t.matmul(p_a8[:, f:f + 1],
                                c_brow[0:1, f * 128:(f + 1) * 128],
                                c_ones[0:1, 0:1],
                                start=False, stop=True)
            S.inc(inst, "m1T", 1)

        @blk.vector
        def _(v):
            S.wait(v, "m1T")
            S.inc(v.tensor_scalar_min(m1_n1[:, :], p_a8[:, 0:8], 0.0),
                  "m1v", 1)
            mk["m1cmin"] = S.val("m1v")
            S.inc(v.tensor_scalar_max(m1_r1[:, :], p_a8[:, 0:8], 0.0), "m1v", 1)
            mk["m1max"] = S.val("m1v")

        @blk.scalar
        def _(sc):
            S.wait(sc, "m1v", mk["m1cmin"])
            i1 = sc.activation(m1_e[:, :], m1_n1[:, :], AF.Exp,
                               bias=0.0, scale=1.0)
            S.inc(i1, "m1a", 1)

        @blk.vector
        def _(v):
            S.wait(v, "m1a")
            S.wait(v, "m1v", mk["m1max"])
            S.inc(v.tensor_add(m1_q[:, :], m1_r1[:, :], m1_e[:, :]), "m1v", 1)
            S.inc(v.memset(m1_se[:, :], 0.0), "m1v", 1)
            S.wait(v, "m1v")
            se = m1_se[:, :].rearrange("p (h d) -> p h d", d=64)
            sn = snaps[:, 0:528].rearrange("p (c s) -> p c s", s=66)
            for half in range(2):
                sl = slice(half * 64, half * 64 + 64)
                S.inc(v.tensor_copy(se[sl, half::2, :], sn[sl, :, 0:64]),
                      "m1v", 1)
            for half in range(2):
                src = snaps[half * 64:(half + 1) * 64,
                            64:64 + 7 * 66 + 1:66]
                dst = pkm[half * 64:(half + 1) * 64,
                          half:half + 7 * 18 + 1:18]
                inst = v.tensor_copy(dst, src)
            S.inc(inst, "m1v", 1)

        @blk.tensor
        def _(t):
            S.wait(t, "m1v")
            S.wait(t, "m1a")
            for h in range(H):
                c, half = h // 2, h % 2
                t.matmul(p_b8[half * 64:half * 64 + 64, c:c + 1],
                         m1_se[:, h * 64:(h + 1) * 64],
                         m1_q[:, c:c + 1],
                         start=(h in (0, 1)), stop=(h in (14, 15)),
                         skip_group_check=True)
            for c in range(8):
                inst = t.matmul(p_n8[:, c:c + 1], pkm[:, 16 * c:16 * c + 16],
                                m1_q[:, c:c + 1],
                                start=(c == 0), stop=(c == 7))
            S.inc(inst, "m1T", 1)

        @blk.vector
        def _(v):
            S.wait(v, "m1T")
            S.inc(v.reduce_sum(m1_rn[:, 0:1], p_n8[:, 0:8], axis=AX.X), "m1v", 1)
            S.wait(v, "m1v")
            S.inc(v.tensor_scalar_add(m1_rn[:, 0:1], m1_rn[:, 0:1], 1e-6),
                  "m1v", 1)
            S.wait(v, "m1v")
            S.inc(v.reciprocal(m1_rn[:, 1:2], m1_rn[:, 0:1]), "m1v", 1)
            S.wait(v, "m1v")
            S.inc(v.tensor_copy(m1_rnb[:, :], m1_rn[:, 1:2]), "m1v", 1)

        @blk.tensor
        def _(t):
            S.wait(t, "m1v")
            for c in range(8):
                inst = t.matmul(p_a8[:, c:c + 1],
                                c_ebc[:, c * 128:(c + 1) * 128],
                                m1_rnb[:, :],
                                start=(c == 0), stop=(c == 7))
            S.inc(inst, "m1T", 1)

        @blk.vector
        def _(v):
            S.wait(v, "m1T")
            S.inc(v.tensor_copy(m1_rbc[:, :], p_a8[:, 0:8]), "m1v", 1)
            S.wait(v, "m1v")
            S.inc(v.tensor_tensor(m1_ao[:, :], p_b8[:, 0:8], m1_rbc[:, :],
                                  ALU.mult), "m1v", 1)

        @blk.tensor
        def _(t):
            S.wait(t, "m1v")
            S.wait(t, "swo")
            for f in range(8):
                for c in range(8):
                    inst = t.matmul(p_b8[:, f:f + 1],
                                    w_out[:, c * 1024 + f * 128:
                                          c * 1024 + (f + 1) * 128],
                                    m1_ao[:, c:c + 1],
                                    start=(c == 0), stop=(c == 7))
            S.inc(inst, "m1T", 1)

        @blk.vector
        def _(v):
            S.wait(v, "m1T")
            S.inc(v.scalar_tensor_tensor(m1_a[:, :], p_b8[:, 0:8], 1.0,
                                         m1_x[:, :], ALU.mult, ALU.add),
                  "m1v", 1)
            S.wait(v, "m1v")
            for c in range(8):
                inst = v.tensor_scalar_add(m1_a[:, c:c + 1], m1_a[:, c:c + 1],
                                           c_outb[:, c:c + 1])
            S.inc(inst, "m1v", 1)
            S.wait(v, "m1v")
            S.inc(v.tensor_copy(m1_ab[:, :], m1_a[:, :]), "m1v", 1)
            S.wait(v, "m1v")
            S.inc(v.tensor_mul(m1_sq[:, :], m1_ab[:, :], m1_ab[:, :]),
                  "m1v", 1)

        @blk.tensor
        def _(t):
            S.wait(t, "m1v")
            for c in range(8):
                t.matmul(p_a8[:, 0:1], c_ones[:, 0:128], m1_ab[:, c:c + 1],
                         start=(c == 0), stop=(c == 7))
            for c in range(8):
                inst = t.matmul(p_a8[:, 1:2], c_ones[:, 0:128],
                                m1_sq[:, c:c + 1],
                                start=(c == 0), stop=(c == 7))
            S.inc(inst, "m1T", 1)

        @blk.vector
        def _(v):
            S.wait(v, "m1T")
            S.inc(v.tensor_scalar_mul(m1_s1[:, :], p_a8[:, 0:1], 1.0 / E),
                  "m1v", 1)
            S.inc(v.tensor_scalar_mul(m1_s2[:, :], p_a8[:, 1:2], 1.0 / E),
                  "m1v", 1)
            S.wait(v, "m1v")
            S.inc(v.tensor_mul(m1_t0[:, :], m1_s1[:, :], m1_s1[:, :]),
                  "m1v", 1)
            S.wait(v, "m1v")
            S.inc(v.scalar_tensor_tensor(m1_t1[:, :], m1_s2[:, :], 1e-5,
                                         m1_t0[:, :], ALU.add,
                                         ALU.subtract), "m1v", 1)
            S.wait(v, "m1v")

        @blk.scalar
        def _(sc):
            S.wait(sc, "m1v")
            S.inc(sc.activation(m1_t1[:, :], m1_t1[:, :], AF.Sqrt,
                                bias=0.0, scale=1.0), "m1a", 1)

        @blk.vector
        def _(v):
            S.wait(v, "m1a")
            S.inc(v.reciprocal(m1_t0[:, :], m1_t1[:, :]), "m1v", 1)
            S.wait(v, "m1v")
            S.inc(v.tensor_scalar(x1m1[:, :], m1_a[:, :], m1_s1[:, 0:1],
                                  None, ALU.subtract), "m1v", 1)
            S.wait(v, "m1v")
            S.inc(v.tensor_scalar_mul(x1m1[:, :], x1m1[:, :], m1_t0[:, 0:1]),
                  "m1v", 1)
            S.wait(v, "m1v")
            S.inc(v.tensor_scalar_mul(x1m1[:, :], x1m1[:, :], c_m0[:, 0:1]),
                  "m1v", 1)
            S.wait(v, "m1v")

        @blk.tensor
        def _(t):
            S.wait(t, "m1v")
            S.wait(t, "m1a")
            S.inc(t.nop(), "p2done", 1)

        es2.close()

        if phases < 3:
            raise _PhaseCut
        es3 = ExitStack()
        p_A = es3.enter_context(nc.psum_tensor("pA", [128, 1024], F32))
        p_C = es3.enter_context(nc.psum_tensor("pC", [128, 1024], F32))
        p_N = es3.enter_context(nc.psum_tensor("pN", [16, 512], F32))
        p_R = es3.enter_context(nc.psum_tensor("pR", [128, 1024], F32))
        qp = mksb(es3, "qp", [128, 8 * 512], BF16)
        qn1 = [mksb(es3, "qn1a", [128, 512], F32),
               mksb(es3, "qn1b", [128, 512], F32)]
        qr1 = [mksb(es3, "qr1a", [128, 512], BF16),
               mksb(es3, "qr1b", [128, 512], BF16)]
        qe = [mksb(es3, "qea", [128, 512], BF16),
              mksb(es3, "qeb", [128, 512], BF16)]
        sbT = mksb(es3, "sbTt", [128, 2048], BF16)
        rn16 = mksb(es3, "rn16", [16, 128], F32)
        rn16b = mksb(es3, "rn16b", [16, 128], BF16)
        rnbc = mksb(es3, "rnbc", [128, 1024], BF16)
        attnb = mksb(es3, "attnb", [128, 8 * 512], BF16)
        ab = [mksb(es3, "ab0", [128, 512], BF16),
              mksb(es3, "ab1", [128, 512], BF16)]
        sqb = [mksb(es3, "sqb0", [128, 512], BF16),
               mksb(es3, "sqb1", [128, 512], BF16)]
        msb = mksb(es3, "msb", [128, 512], F32)
        vsb = mksb(es3, "vsb", [128, 512], F32)
        s2sb = qn1[0]
        af = mksb(es3, "af", [128, 8 * 512], F32)
        xres = [mksb(es3, "xr0", [128, 512], F32),
                mksb(es3, "xr1", [128, 512], F32)]
        x1c = [qn1[1], qn1[1]]

        @blk.vector
        def _(v):
            S.wait(v, "m1v")
            S.wait(v, "m1a")
            S.inc(v.memset(kexp[:, :], 0.0), "kexpz", 1)
            S.inc(v.memset(sexp[:, :], 0.0), "kexpz", 1)

        for tile in range(NTILE):
            emit_xload(tile, "own", ("main", tile))

            for fp in range(4):
                for fo in range(2):
                    f = 2 * fp + fo
                    bs = (p_A, p_A, p_C, p_C)[f % 4][
                        :, (f % 2) * 512:(f % 2) * 512 + 512]
                    mk[("qbs", f % 2)] = bs

                    @blk.tensor
                    def _(t, f=f, bs=bs, tile=tile):
                        S.wait(t, "xc", mk["xc"][("main", tile)])
                        if tile == 0 and f == 0:
                            S.wait(t, "p2done")
                        if f < 4:
                            S.wait(t, "resv", S.val("resv"))
                            S.wait(t, "lnv", S.val("lnv"))
                            S.wait(t, "lapp", S.val("lapp"))
                            S.wait(t, "x1st0", S.val("x1st0"))
                            S.wait(t, "x1st1", S.val("x1st1"))
                        else:
                            S.wait(t, "qelu", mk[("qmax", tile, f - 4)])
                        par = tile % 2
                        for c in range(8):
                            t.matmul(bs,
                                     w_q[:, c * 1024 + f * 128:
                                         c * 1024 + (f + 1) * 128],
                                     xb[par][:, c * 512:(c + 1) * 512],
                                     start=(c == 0), stop=False)
                        inst = t.matmul(bs,
                                        c_brow[0:1, f * 128:(f + 1) * 128],
                                        c_ones[0:1, 0:512],
                                        start=False, stop=True)
                        S.inc(inst, "qprojT", 1)

                    @blk.vector
                    def _(v, f=f, fo=fo, bs=bs, tile=tile):
                        S.wait(v, "qprojT")
                        S.wait(v, "qelua")
                        S.inc(v.tensor_scalar_min(qn1[fo][:, :], bs, 0.0),
                              "qelu", 1)
                        mk[("qcmin", tile, f)] = S.val("qelu")
                        S.inc(v.tensor_scalar_max(qr1[fo][:, :], bs, 0.0),
                              "qelu", 1)
                        mk[("qmax", tile, f)] = S.val("qelu")

                @blk.scalar
                def _(sc, fp=fp, tile=tile):
                    for fo in range(2):
                        S.wait(sc, "qelu", mk[("qcmin", tile, 2 * fp + fo)])
                        S.inc(sc.activation(qe[fo][:, :], qn1[fo][:, :],
                                            AF.Exp, bias=0.0, scale=1.0),
                              "qelua", 1)

                for fo in range(2):
                    @blk.vector
                    def _(v, f=2 * fp + fo, fo=fo):
                        S.wait(v, "qelua", S.val("qelua") - (1 - fo))
                        S.wait(v, "qelu", mk[("qmax", tile, 2 * fp + fo)])
                        S.inc(v.tensor_add(qp[:, f * 512:(f + 1) * 512],
                                           qr1[fo][:, :], qe[fo][:, :]),
                              "qelu", 1)

            @blk.tensor
            def _(t):
                S.inc(t.nop(), "xbfree", 1)

            for jj in range(4 if p3sub >= 2 else 0):
                jglob = tile * 4 + jj

                @blk.vector
                def _(v, jglob=jglob):
                    S.wait(v, "qelu")
                    S.wait(v, "normT", S.val("normT"))
                    if jglob == 0:
                        S.wait(v, "kexpz")
                    S.wait(v, "scoreT", S.val("scoreT"))
                    S.wait(v, "svT", S.val("svT"))
                    ke = kexp[:, :].rearrange("p (h t) -> p h t", t=128)
                    kf = kfm[:, :].rearrange("p (c t) -> p c t", t=T)
                    se = sexp[:, :].rearrange("p (h d) -> p h d", d=64)
                    sn = snaps[:, jglob * 528:(jglob + 1) * 528].rearrange(
                        "p (c s) -> p c s", s=66)
                    for half in range(2):
                        sl = slice(half * 64, half * 64 + 64)
                        S.inc(v.tensor_copy(
                            ke[sl, half::2, :],
                            kf[sl, :, jglob * 128:(jglob + 1) * 128]),
                            "pkmv", 1)
                        S.inc(v.tensor_copy(se[sl, half::2, :],
                                            sn[sl, :, 0:64]), "pkmv", 1)
                    if p3sub != 22:
                        for half in range(2):
                            src = snaps[half * 64:(half + 1) * 64,
                                        jglob * 528 + 64:
                                        jglob * 528 + 64 + 7 * 66 + 1:66]
                            dst = pkm[half * 64:(half + 1) * 64,
                                      half:half + 7 * 18 + 1:18]
                            inst = v.tensor_copy(dst, src)
                    else:
                        inst = v.memset(pkm[:, 0:1], 0.0)
                    S.inc(inst, "pkmv", 1)

                for wave in range(4 if p3sub != 21 else 0):
                    bnk = (wave % 2) * 512

                    @blk.tensor
                    def _(t, jglob=jglob, jj=jj, wave=wave, bnk=bnk):
                        if wave == 0:
                            S.wait(t, "qelu")
                            S.wait(t, "pkmv")
                        S.wait(t, "maskv", max(0, S.val("maskv") - 1))
                        for i in range(4):
                            h = wave * 4 + i
                            c = h // 2
                            inst = t.matmul(
                                p_A[:, bnk + i * 128:bnk + (i + 1) * 128],
                                kexp[:, h * 128:(h + 1) * 128],
                                qp[:, c * 512 + jj * 128:
                                   c * 512 + jj * 128 + 128],
                                start=(i == 0), stop=(i == 3))
                        S.inc(inst, "scoreT", 1)

                    @blk.vector
                    def _(v, wave=wave, bnk=bnk):
                        S.wait(v, "scoreT")
                        S.wait(v, "svT", S.val("svT"))
                        S.wait(v, "normT", S.val("normT"))
                        if p3sub == 23:
                            inst = v.memset(sbT[:, 0:1], 0.0)
                        else:
                            inst = v.tensor_tensor(
                                sbT[:, wave * 512:(wave + 1) * 512],
                                p_A[:, bnk:bnk + 512], c_mask[:, 0:512],
                                ALU.mult)
                        S.inc(inst, "maskv", 1)

                if p3sub < 3:
                    continue

                if mk.get("pendrn") is not None:
                    mk["pendrn"]()
                    mk["pendrn"] = None
                if mk.get("pend3") is not None:
                    mk["pend3"]()
                    mk["pend3"] = None

                @blk.tensor
                def _(t, jglob=jglob, jj=jj):
                    S.wait(t, "pkmv")
                    S.wait(t, "aov", S.val("aov"))
                    S.wait(t, "lnv", S.val("lnv"))
                    for h in range(H):
                        c, half = h // 2, h % 2
                        t.matmul(
                            p_C[half * 64:half * 64 + 64,
                                c * 128:(c + 1) * 128],
                            sexp[:, h * 64:(h + 1) * 64],
                            qp[:, c * 512 + jj * 128:c * 512 + jj * 128 + 128],
                            start=(h in (0, 1, 8, 9)), stop=False,
                            skip_group_check=True)
                    S.wait(t, "rn16v", S.val("rn16v"))
                    for c in range(8):
                        t.matmul(p_N[:, 0:128],
                                 pkm[:, 16 * c:16 * c + 16],
                                 qp[:, c * 512 + jj * 128:
                                    c * 512 + jj * 128 + 128],
                                 start=(c == 0), stop=False)
                    S.wait(t, "maskv")
                    for h in list(range(0, H, 2)) + list(range(1, H, 2)):
                        c, half = h // 2, h % 2
                        inst = t.matmul(p_C[half * 64:half * 64 + 64,
                                            c * 128:(c + 1) * 128],
                                        vaug[:, jglob * H * 66 + h * 66:
                                             jglob * H * 66 + h * 66 + 64],
                                        sbT[:, h * 128:(h + 1) * 128],
                                        start=False,
                                        stop=(h in (6, 7, 14, 15)),
                                        skip_group_check=True)
                    S.inc(inst, "svT", 1)
                    for h in range(H):
                        inst = t.matmul(p_N[:, 0:128],
                                        c_e31[:, 15 - h:31 - h],
                                        sbT[:, h * 128:(h + 1) * 128],
                                        start=False, stop=(h == 15))
                    S.inc(inst, "normT", 1)

                def rnchain3():
                    @blk.vector
                    def _(v):
                        S.wait(v, "normT")
                        S.wait(v, "rnT", S.val("rnT"))
                        S.inc(v.tensor_scalar_add(rn16[:, :], p_N[:, 0:128],
                                                  1e-6), "rn16v", 1)
                        S.wait(v, "rn16v")
                        S.inc(v.reciprocal(rn16[:, :], rn16[:, :]),
                              "rn16v", 1)
                        S.wait(v, "rn16v")
                        S.inc(v.tensor_copy(rn16b[:, :], rn16[:, :]),
                              "rn16v", 1)

                def tail3(jj=jj):
                    @blk.tensor
                    def _(t):
                        S.wait(t, "rn16v")
                        S.wait(t, "aov", S.val("aov"))
                        for c in range(8):
                            inst = t.matmul(p_R[:, c * 128:(c + 1) * 128],
                                            c_ebc[:, c * 128:(c + 1) * 128],
                                            rn16b[:, :],
                                            start=(c in (0, 4)),
                                            stop=(c in (3, 7)))
                        S.inc(inst, "rnT", 1)

                    @blk.vector
                    def _(v):
                        S.wait(v, "rnT")
                        S.wait(v, "svT")
                        S.wait(v, "oprojT", S.val("oprojT"))
                        adst = attnb[:, :].rearrange("p (c t) -> p c t",
                                                     t=512)
                        for b in range(2):
                            sl = slice(b * 512, (b + 1) * 512)
                            S.inc(v.tensor_copy(rnbc[:, sl], p_R[:, sl]),
                                  "rnc", 1)
                            S.wait(v, "rnc")
                            inst = v.tensor_tensor(
                                adst[:, b * 4:(b + 1) * 4,
                                     jj * 128:(jj + 1) * 128],
                                p_C[:, sl].rearrange("p (c t) -> p c t",
                                                     t=128),
                                rnbc[:, sl].rearrange("p (c t) -> p c t",
                                                      t=128),
                                ALU.mult)
                            S.inc(inst, "aov", 1)

                mk["pendrn"] = rnchain3
                mk["pend3"] = tail3

            if mk.get("pendrn") is not None:
                mk["pendrn"]()
                mk["pendrn"] = None
            if mk.get("pend3") is not None:
                mk["pend3"]()
                mk["pend3"] = None

            for f in range(8 if p3sub >= 4 else 0):
                bs = (p_A, p_A, p_C, p_C)[f % 4][
                    :, (f % 2) * 512:(f % 2) * 512 + 512]

                @blk.sync
                def _(sync, f=f, tile=tile):
                    if tile == 0 and f == 0:
                        S.wait(sync, "p2done")
                    if f >= 2:
                        S.wait(sync, "resv", S.val("resv") - 1)
                    S.inc(sync.dma_start(
                        out=xres[f % 2][:, :],
                        in_=d_xsl[f * 128:(f + 1) * 128,
                                  tile * 512:(tile + 1) * 512]),
                        "xrl%d" % (f % 2), 16)

                @blk.tensor
                def _(t, f=f, bs=bs, tile=tile):
                    S.wait(t, "aov")
                    if f < 2:
                        S.wait(t, "maskv", S.val("maskv"))
                    elif f >= 4:
                        S.wait(t, "resv", mk[("resv", tile, f - 4)])
                    for c in range(8):
                        inst = t.matmul(bs,
                                        w_out[:, c * 1024 + f * 128:
                                              c * 1024 + (f + 1) * 128],
                                        attnb[:, c * 512:(c + 1) * 512],
                                        start=(c == 0), stop=(c == 7))
                    S.inc(inst, "oprojT", 1)

                @blk.vector
                def _(v, f=f, bs=bs):
                    S.wait(v, "oprojT")
                    S.wait(v, "xrl%d" % (f % 2))
                    i0 = v.scalar_tensor_tensor(af[:, f * 512:(f + 1) * 512],
                                                bs, c_outb[:, f:f + 1],
                                                xres[f % 2][:, :],
                                                ALU.add, ALU.add)
                    S.inc(i0, "resv", 1)
                    mk[("resv", tile, f)] = S.val("resv")

            for c in range(8 if p3sub >= 5 else 0):
                @blk.vector
                def _(v, c=c):
                    if c < 2:
                        S.wait(v, "resv")
                    if c >= 2:
                        S.wait(v, "statT", S.val("statT"))
                    S.inc(v.tensor_copy(ab[c % 2][:, :],
                                        af[:, c * 512:(c + 1) * 512]),
                          "abv", 1)

                @blk.gpsimd
                def _(g, c=c):
                    S.wait(g, "abv")
                    S.inc(g.tensor_mul(sqb[c % 2][:, :], ab[c % 2][:, :],
                                       ab[c % 2][:, :]), "sqp", 1)

                @blk.tensor
                def _(t, c=c):
                    S.wait(t, "abv")
                    if c == 0:
                        S.wait(t, "lnv", S.val("lnv"))
                    t.matmul(p_C[:, 0:512], c_ones[:, 0:128], ab[c % 2][:, :],
                             start=(c == 0), stop=(c == 7))
                    S.wait(t, "sqp")
                    inst = t.matmul(p_C[:, 512:1024], c_ones[:, 0:128],
                                    sqb[c % 2][:, :],
                                    start=(c == 0), stop=(c == 7))
                    S.inc(inst, "statT", 1)

            if p3sub < 5:
                continue

            @blk.vector
            def _(v):
                S.wait(v, "statT")
                S.inc(v.tensor_scalar_mul(msb[:, :], p_C[:, 0:512], 1.0 / E),
                      "lnv", 1)
                S.inc(v.tensor_scalar_mul(s2sb[:, :], p_C[:, 512:1024],
                                          1.0 / E), "lnv", 1)
                S.wait(v, "lnv")
                S.inc(v.tensor_mul(vsb[:, :], msb[:, :], msb[:, :]), "lnv", 1)
                S.wait(v, "lnv")
                S.inc(v.scalar_tensor_tensor(vsb[:, :], s2sb[:, :], 1e-5,
                                             vsb[:, :], ALU.add,
                                             ALU.subtract), "lnv", 1)
                S.wait(v, "lnv")

            @blk.scalar
            def _(sc):
                S.wait(sc, "lnv")
                S.inc(sc.activation(s2sb[:, :], vsb[:, :], AF.Sqrt,
                                    bias=0.0, scale=1.0), "lna", 1)

            @blk.vector
            def _(v):
                S.wait(v, "lna")
                S.inc(v.reciprocal(vsb[:, :], s2sb[:, :]), "lnv", 1)
                S.wait(v, "lnv")

            for c in range(8):
                @blk.vector
                def _(v, c=c, tile=tile):
                    S.wait(v, "lnv")
                    S.wait(v, "x1st0")
                    S.wait(v, "x1st1")
                    S.inc(v.tensor_sub(x1c[c % 2][:, :],
                                       af[:, c * 512:(c + 1) * 512],
                                       msb[:, :]), "lapp", 1)

                @blk.gpsimd
                def _(g, c=c):
                    S.wait(g, "lapp")
                    S.inc(g.tensor_mul(x1c[c % 2][:, :], x1c[c % 2][:, :],
                                       vsb[:, :]), "lappp", 1)

                @blk.vector
                def _(v, c=c, tile=tile):
                    S.wait(v, "lappp")
                    S.inc(v.tensor_copy(xlast[:, tile * 8 + c:
                                              tile * 8 + c + 1],
                                        x1c[c % 2][:, 511:512]), "x1v", 1)

                @blk.sync
                def _(sync, c=c, tile=tile):
                    S.wait(sync, "x1v")
                    S.inc(sync.dma_start(
                        out=d_x1[c, :, tile * 512:(tile + 1) * 512],
                        in_=x1c[c % 2][:, :]), "x1st%d" % (c % 2), 16)

        @blk.tensor
        def _(t):
            S.wait(t, "lnv", S.val("lnv"))
            S.wait(t, "lappp", S.val("lappp"))
            S.wait(t, "rn16v", S.val("rn16v"))
            S.wait(t, "rnbcv", S.val("rnbcv"))
            S.wait(t, "aov", S.val("aov"))
            S.wait(t, "maskv", S.val("maskv"))
            S.wait(t, "resv", S.val("resv"))
            S.inc(t.nop(), "p3done", 1)

        es3.close()
        es13.close()

        if phases < 4:
            raise _PhaseCut
        es4 = ExitStack()
        p_F = es4.enter_context(nc.psum_tensor("pF", [128, 1024], F32))
        p_G = es4.enter_context(nc.psum_tensor("pG", [128, 1024], F32))
        w_ffn = mksb(es4, "w_ffn", [128, 3 * 8 * 1024], BF16)
        x1g = [mksb(es4, "x1g0", [128, 8 * 512], F32),
               mksb(es4, "x1g1", [128, 8 * 512], F32)]
        x1p = [mksb(es4, "x1p0", [128, 8 * 512], F32),
               mksb(es4, "x1p1", [128, 8 * 512], F32)]
        gh = mksb(es4, "gh", [128, 8], F32)
        dtl = [mksb(es4, "dtl0", [128, 512], F32),
               mksb(es4, "dtl1", [128, 512], F32)]
        xrb = [mksb(es4, "xrb0", [128, 8 * 512], BF16),
               mksb(es4, "xrb1", [128, 8 * 512], BF16)]
        xkb = [mksb(es4, "xkb0", [128, 8 * 512], BF16),
               mksb(es4, "xkb1", [128, 8 * 512], BF16)]
        rb = mksb(es4, "rb", [128, 8 * 512], BF16)
        kab = mksb(es4, "kab", [128, 8 * 512], BF16)
        a2 = mksb(es4, "a2", [128, 8 * 512], F32)
        ab2r = [mksb(es4, "ab2r0", [128, 512], BF16),
                mksb(es4, "ab2r1", [128, 512], BF16)]
        sqr = [mksb(es4, "sqr0", [128, 512], BF16),
               mksb(es4, "sqr1", [128, 512], BF16)]
        ms2 = mksb(es4, "ms2", [128, 512], F32)
        s2s2 = mksb(es4, "s2s2", [128, 512], F32)
        vs2 = mksb(es4, "vs2", [128, 512], F32)

        @blk.gpsimd
        def _(g):
            S.wait(g, "p3done")
            for wi, dw in enumerate([d_wrt, d_wkt, d_wvt]):
                for c in range(8):
                    S.inc(g.dma_start(
                        out=w_ffn[:, wi * 8192 + c * 1024:
                                  wi * 8192 + (c + 1) * 1024],
                        in_=dw[c * 128:(c + 1) * 128, :]), "swf", 16)

        mk4 = dict(x1ld={}, mixv={}, mixp={}, rkT={}, a2v={}, yst={},
                   l2appp={}, ln2v={})

        def emit_ffn_head(tile):
            par = tile % 2

            @blk.sync
            def _(sync):
                S.wait(sync, "x1st0")
                S.wait(sync, "x1st1")
                if tile >= 2:
                    S.wait(sync, "a2v", mk4["a2v"][tile - 2])
                for c in range(8):
                    S.inc(sync.dma_start(
                        out=x1g[par][:, c * 512:(c + 1) * 512],
                        in_=d_x1[c, :, tile * 512:(tile + 1) * 512]),
                        "x1ld", 16)

            mk4["x1ld"][tile] = S.val("x1ld")

            @blk.vector
            def _(v):
                S.wait(v, "x1ld", mk4["x1ld"][tile])
                if tile >= 2:
                    S.wait(v, "yst", mk4["yst"][tile - 2])
                    S.wait(v, "rkT", mk4["rkT"][tile - 2])
                if tile == 0:
                    S.inc(v.tensor_copy(gh[:, :], x1m1[:, :]), "mixv", 1)
                else:
                    S.inc(v.tensor_copy(gh[:, :],
                                        xlast[:, (tile - 1) * 8:tile * 8]),
                          "mixv", 1)
                S.wait(v, "mixv")
                xg = x1g[par][:, :].rearrange("p (c t) -> p c t", t=512)
                xp = x1p[par][:, :].rearrange("p (c t) -> p c t", t=512)
                S.inc(v.tensor_copy(xp[:, :, 1:512], xg[:, :, 0:511]),
                      "mixv", 1)
                S.inc(v.tensor_copy(
                    xp[:, :, 0:1],
                    gh[:, :].rearrange("p (c o) -> p c o", o=1)), "mixv", 1)
                S.wait(v, "mixv")
                for c in range(8):
                    gi = tile * 8 + c
                    S.inc(v.tensor_sub(dtl[gi % 2][:, :],
                                       x1g[par][:, c * 512:(c + 1) * 512],
                                       x1p[par][:, c * 512:(c + 1) * 512]),
                          "mixv", 1)
                    S.wait(v, "mixv")
                    S.inc(v.scalar_tensor_tensor(
                        xrb[par][:, c * 512:(c + 1) * 512],
                        dtl[gi % 2][:, :], c_mur[:, c:c + 1],
                        x1p[par][:, c * 512:(c + 1) * 512],
                        ALU.mult, ALU.add), "mixv", 1)
                    S.inc(v.scalar_tensor_tensor(
                        xkb[par][:, c * 512:(c + 1) * 512],
                        dtl[gi % 2][:, :], c_muk[:, c:c + 1],
                        x1p[par][:, c * 512:(c + 1) * 512],
                        ALU.mult, ALU.add), "mixv", 1)

            mk4["mixv"][tile] = S.val("mixv")

        def emit_ffn_body(tile):
            par = tile % 2
            for f in range(8):
                bA = p_F[:, (f % 2) * 512:(f % 2) * 512 + 512]
                bB = p_G[:, (f % 2) * 512:(f % 2) * 512 + 512]

                @blk.tensor
                def _(t, f=f, bA=bA, bB=bB):
                    if f == 0:
                        S.wait(t, "mixv", mk4["mixv"][tile])
                        if tile == 0:
                            S.wait(t, "swf")
                        else:
                            S.wait(t, "a2v", mk4["a2v"][tile - 1])
                            S.wait(t, "ln2v", mk4["ln2v"][tile - 1])
                    if f >= 2:
                        S.wait(t, "sigm", mk4[("sig", tile, f - 2)])
                        S.wait(t, "kactv", mk4[("kact", tile, f - 2)])
                    for c in range(8):
                        t.matmul(bA,
                                 w_ffn[:, c * 1024 + f * 128:
                                       c * 1024 + (f + 1) * 128],
                                 xrb[par][:, c * 512:(c + 1) * 512],
                                 start=(c == 0), stop=(c == 7))
                    for c in range(8):
                        inst = t.matmul(
                            bB,
                            w_ffn[:, 8192 + c * 1024 + f * 128:
                                  8192 + c * 1024 + (f + 1) * 128],
                            xkb[par][:, c * 512:(c + 1) * 512],
                            start=(c == 0), stop=(c == 7))
                    S.inc(inst, "rkT", 1)

                @blk.scalar
                def _(sc, f=f, bA=bA):
                    S.wait(sc, "rkT")
                    S.inc(sc.activation(rb[:, f * 512:(f + 1) * 512], bA,
                                        AF.Sigmoid, bias=0.0, scale=1.0),
                          "sigm", 1)
                    mk4[("sig", tile, f)] = S.val("sigm")

                @blk.vector
                def _(v, f=f, bB=bB):
                    S.wait(v, "rkT")
                    S.inc(v.tensor_scalar_max(kab[:, f * 512:(f + 1) * 512],
                                              bB, 0.0), "kactv", 1)
                    S.wait(v, "kactv")
                    S.inc(v.tensor_mul(kab[:, f * 512:(f + 1) * 512],
                                       kab[:, f * 512:(f + 1) * 512],
                                       kab[:, f * 512:(f + 1) * 512]),
                          "kactv", 1)
                    mk4[("kact", tile, f)] = S.val("kactv")

            mk4["rkT"][tile] = S.val("rkT")
            sig_end = S.val("sigm")
            kact_end = S.val("kactv")

            for f in range(8):
                bA = p_F[:, (f % 2) * 512:(f % 2) * 512 + 512]

                @blk.tensor
                def _(t, f=f, bA=bA):
                    if f == 0:
                        S.wait(t, "sigm", sig_end)
                        S.wait(t, "kactv", kact_end)
                    if f >= 2:
                        S.wait(t, "a2v", mk4[("a2mul", tile, f - 2)])
                    for c in range(8):
                        inst = t.matmul(
                            bA,
                            w_ffn[:, 16384 + c * 1024 + f * 128:
                                  16384 + c * 1024 + (f + 1) * 128],
                            kab[:, c * 512:(c + 1) * 512],
                            start=(c == 0), stop=(c == 7))
                    S.inc(inst, "fvT", 1)

                if mk4.get("pendst") is not None:
                    mk4["pendst"]()
                    mk4["pendst"] = None

                @blk.vector
                def _(v, f=f, bA=bA):
                    S.wait(v, "fvT")
                    if S.val("st2T") >= 1:
                        S.wait(v, "st2T", S.val("st2T") - 1)
                    S.inc(v.tensor_tensor(a2[:, f * 512:(f + 1) * 512], bA,
                                          rb[:, f * 512:(f + 1) * 512],
                                          ALU.mult), "a2v", 1)
                    mk4[("a2mul", tile, f)] = S.val("a2v")
                    S.wait(v, "a2v")
                    S.inc(v.tensor_add(a2[:, f * 512:(f + 1) * 512],
                                       a2[:, f * 512:(f + 1) * 512],
                                       x1g[par][:, f * 512:(f + 1) * 512]),
                          "a2v", 1)
                    mk4[("a2add", tile, f)] = S.val("a2v")
                    S.wait(v, "a2v")
                    S.inc(v.tensor_copy(ab2r[f % 2][:, :],
                                        a2[:, f * 512:(f + 1) * 512]),
                          "a2v", 1)

                @blk.scalar
                def _(sc, f=f):
                    S.wait(sc, "a2v", mk4[("a2add", tile, f)])
                    if S.val("st2T") >= 1:
                        S.wait(sc, "st2T", S.val("st2T") - 1)
                    S.inc(sc.activation(sqr[f % 2][:, :],
                                        a2[:, f * 512:(f + 1) * 512],
                                        AF.Square, bias=0.0, scale=1.0),
                          "sqp2", 1)

                def st_blk(f=f):
                    @blk.tensor
                    def _(t):
                        S.wait(t, "a2v")
                        S.wait(t, "sqp2")
                        t.matmul(p_G[:, 0:512], c_ones[:, 0:128],
                                 ab2r[f % 2][:, :], start=(f == 0),
                                 stop=(f == 7), skip_group_check=True)
                        inst = t.matmul(p_G[:, 512:1024], c_ones[:, 0:128],
                                        sqr[f % 2][:, :], start=(f == 0),
                                        stop=(f == 7), skip_group_check=True)
                        S.inc(inst, "st2T", 1)

                mk4["pendst"] = st_blk

            if mk4.get("pendst") is not None:
                mk4["pendst"]()
                mk4["pendst"] = None

            mk4["a2v"][tile] = S.val("a2v")

            @blk.vector
            def _(v):
                S.wait(v, "st2T")
                if tile >= 1:
                    S.wait(v, "l2appp", mk4["l2appp"][tile - 1])
                S.inc(v.tensor_scalar_mul(ms2[:, :], p_G[:, 0:512], 1.0 / E),
                      "ln2v", 1)
                S.inc(v.tensor_scalar_mul(s2s2[:, :], p_G[:, 512:1024],
                                          1.0 / E), "ln2v", 1)
                S.wait(v, "ln2v")
                S.inc(v.tensor_mul(vs2[:, :], ms2[:, :], ms2[:, :]),
                      "ln2v", 1)
                S.wait(v, "ln2v")
                S.inc(v.scalar_tensor_tensor(vs2[:, :], s2s2[:, :], 1e-5,
                                             vs2[:, :], ALU.add,
                                             ALU.subtract), "ln2v", 1)
                S.wait(v, "ln2v")

            @blk.scalar
            def _(sc):
                S.wait(sc, "ln2v")
                S.inc(sc.activation(s2s2[:, :], vs2[:, :], AF.Sqrt,
                                    bias=0.0, scale=1.0), "ln2a", 1)

            @blk.vector
            def _(v):
                S.wait(v, "ln2a")
                S.inc(v.reciprocal(vs2[:, :], s2s2[:, :]), "ln2v", 1)

            mk4["ln2v"][tile] = S.val("ln2v")

            for c in range(8):
                @blk.vector
                def _(v, c=c):
                    S.wait(v, "ln2v")
                    S.inc(v.tensor_sub(x1p[par][:, c * 512:(c + 1) * 512],
                                       a2[:, c * 512:(c + 1) * 512],
                                       ms2[:, :]), "l2app", 1)

                @blk.gpsimd
                def _(g_, c=c):
                    S.wait(g_, "l2app")
                    S.inc(g_.tensor_mul(x1p[par][:, c * 512:(c + 1) * 512],
                                        x1p[par][:, c * 512:(c + 1) * 512],
                                        vs2[:, :]), "l2appp", 1)

            mk4["l2appp"][tile] = S.val("l2appp")

            @blk.sync
            def _(sync):
                S.wait(sync, "l2appp", mk4["l2appp"][tile])
                for c in range(8):
                    S.inc(sync.dma_start(
                        out=d_y[c, :, tile * 512:(tile + 1) * 512],
                        in_=x1p[par][:, c * 512:(c + 1) * 512]), "yst", 16)

            mk4["yst"][tile] = S.val("yst")

        emit_ffn_head(0)
        for tile in range(NTILE):
            if tile + 1 < NTILE:
                emit_ffn_head(tile + 1)
            emit_ffn_body(tile)

        es4.close()

        @blk.sync
        def _(sync):
            S.wait(sync, "yst")
    except _PhaseCutExc:
        pass

    return nc



def _prep_inputs(inputs, cfg: Cfg):
    x = np.ascontiguousarray(np.asarray(inputs["L0_attn_x"], np.float32)[0])
    xT = np.ascontiguousarray(x.T)
    T, PCH = cfg.T, cfg.PCH
    qkv_w = np.ascontiguousarray(np.asarray(inputs["L0_attn_qkv_w"], np.float32))
    out_w = np.ascontiguousarray(np.asarray(inputs["L0_attn_out_w"], np.float32))
    wrT = np.ascontiguousarray(np.asarray(inputs["L0_ffn_W_r"], np.float32).T)
    wkT = np.ascontiguousarray(np.asarray(inputs["L0_ffn_W_k"], np.float32).T)
    wvT = np.ascontiguousarray(np.asarray(inputs["L0_ffn_W_v"], np.float32).T)
    qkv_b = np.asarray(inputs["L0_attn_qkv_b"], np.float32)
    out_b = np.asarray(inputs["L0_attn_out_b"], np.float32)
    mu_r = np.asarray(inputs["L0_ffn_mu_r"], np.float32)
    mu_k = np.asarray(inputs["L0_ffn_mu_k"], np.float32)

    brow = np.ascontiguousarray(qkv_b[None, :])
    outb_fm = np.ascontiguousarray(out_b.reshape(8, 128).T)
    mur_fm = np.ascontiguousarray(mu_r.reshape(8, 128).T)
    muk_fm = np.ascontiguousarray(mu_k.reshape(8, 128).T)

    ident = np.eye(128, dtype=np.float32).astype(BB)
    maskc = np.tile(np.triu(np.ones((CH, CH), np.float32)), (1, 4)).astype(BB)
    e31 = np.zeros((128, 31), np.float32)
    e31[:, 15] = 1.0
    e31 = e31.astype(BB)
    ebc = np.zeros((16, 8 * 128), np.float32)
    for c in range(8):
        for m in range(128):
            ebc[2 * c + m // 64, c * 128 + m] = 1.0
    ebc = ebc.astype(BB)
    ones = np.ones((128, 512), np.float32).astype(BB)

    in_maps = []
    for i in range(cfg.ncores):
        start = i * T
        xpfx = np.zeros((E, PCH * CH), np.float32)
        nreal = min(start, PCH * CH)
        if nreal > 0:
            xpfx[:, PCH * CH - nreal:] = xT[:, start - nreal:start]
        xsl = np.ascontiguousarray(xT[:, start:start + T])
        if start > 0:
            xm1 = np.ascontiguousarray(xT[:, start - 1].reshape(8, 128).T)
        else:
            xm1 = np.zeros((128, 8), np.float32)
        pmask = np.zeros((128, PCH), np.float32)
        nrealch = nreal // CH
        if nrealch > 0:
            pmask[:, PCH - nrealch:] = 1.0
        m0 = np.full((128, 1), 0.0 if i == 0 else 1.0, np.float32)
        in_maps.append(dict(
            xpfx=xpfx, xsl=xsl, xm1=xm1, pmask=pmask, m0=m0,
            qkv_w=qkv_w, out_w=out_w, wrT=wrT, wkT=wkT, wvT=wvT,
            brow=brow, outb_fm=outb_fm, mur_fm=mur_fm, muk_fm=muk_fm,
            ident=ident, maskc=maskc, e31=e31, ebc=ebc, ones=ones,
        ))
    return in_maps


def kernel(**inputs):
    cfg = Cfg()
    in_maps = _prep_inputs(inputs, cfg)
    nc = build_nc(cfg)
    res = run_bass_kernel_spmd(nc, in_maps, core_ids=list(range(cfg.ncores)))
    outs = []
    for i in range(cfg.ncores):
        y = np.asarray(res.results[i]["y"])
        outs.append(y.transpose(2, 0, 1).reshape(cfg.T, E))
    return np.concatenate(outs, axis=0)[None].astype(np.float32)


# version 23
# speedup vs baseline: 1.0058x; 1.0058x over previous
import numpy as np
import ml_dtypes
from contextlib import ExitStack

import concourse.bass as bass
import concourse.mybir as mybir
from concourse.bass_utils import run_bass_kernel_spmd

F32 = mybir.dt.float32
BF16 = mybir.dt.bfloat16
AF = mybir.ActivationFunctionType
ALU = mybir.AluOpType
AX = mybir.AxisListType
BB = ml_dtypes.bfloat16

E = 1024
H = 16
CH = 128


class Cfg:
    def __init__(self, T=2048, PCH=112, ncores=8):
        self.T = T
        self.PCH = PCH
        self.NTILE = T // 512
        self.NCHUNK = T // CH
        self.ncores = ncores


class Sems:
    def __init__(self, nc, es):
        self.nc, self.es = nc, es
        self.h, self.n = {}, {}

    def _get(self, name):
        if name not in self.h:
            self.h[name] = self.es.enter_context(self.nc.semaphore(name))
            self.n[name] = 0
        return self.h[name]

    def inc(self, inst, name, k=1):
        inst.then_inc(self._get(name), k)
        self.n[name] += k
        return self.n[name]

    def wait(self, eng, name, val=None):
        eng.wait_ge(self._get(name), self.n[name] if val is None else val)

    def val(self, name):
        self._get(name)
        return self.n[name]


class _PhaseCutExc(Exception):
    pass


def build_nc(cfg: Cfg, phases=4, p3sub=9, pskip=()):
    _PhaseCut = _PhaseCutExc
    nc = bass.Bass("TRN2", target_bir_lowering=False)
    T, PCH, NTILE, NCHUNK = cfg.T, cfg.PCH, cfg.NTILE, cfg.NCHUNK
    NPT = PCH // 4

    def din(name, shape, dtype):
        return nc.dram_tensor(name, shape, dtype, kind="ExternalInput").ap()

    d_xpfx = din("xpfx", [E, PCH * CH], F32)
    d_xsl = din("xsl", [E, T], F32)
    d_xm1 = din("xm1", [128, 8], F32)
    d_pmask = din("pmask", [128, PCH], F32)
    d_m0 = din("m0", [128, 1], F32)
    d_qkvw = din("qkv_w", [E, 3 * E], F32)
    d_outw = din("out_w", [E, E], F32)
    d_wrt = din("wrT", [E, E], F32)
    d_wkt = din("wkT", [E, E], F32)
    d_wvt = din("wvT", [E, E], F32)
    d_brow = din("brow", [1, 3 * E], F32)
    d_outb = din("outb_fm", [128, 8], F32)
    d_mur = din("mur_fm", [128, 8], F32)
    d_muk = din("muk_fm", [128, 8], F32)
    d_ident = din("ident", [128, 128], BF16)
    d_mask = din("maskc", [128, 4 * CH], BF16)
    d_e31 = din("e31", [128, 31], BF16)
    d_ebc = din("ebc", [16, 8 * 128], BF16)
    d_ones = din("ones", [128, 512], BF16)
    d_y = nc.dram_tensor("y", [8, 128, T], F32, kind="ExternalOutput").ap()
    d_x1 = nc.dram_tensor("x1d", [8, 128, T], F32).ap()

    es = ExitStack()
    try:
      with es:
        S = Sems(nc, es)
        blk = es.enter_context(nc.Block())

        def mksb(stack, name, shape, dtype):
            return stack.enter_context(nc.sbuf_tensor(name, shape, dtype))

        c_id = mksb(es, "c_id", [128, 128], BF16)
        c_mask = mksb(es, "c_mask", [128, 4 * CH], BF16)
        c_e31 = mksb(es, "c_e31", [128, 31], BF16)
        c_ebc = mksb(es, "c_ebc", [16, 8 * 128], BF16)
        c_ones = mksb(es, "c_ones", [128, 512], BF16)
        c_brow = mksb(es, "c_brow", [1, E], BF16)
        c_outb = mksb(es, "c_outb", [128, 8], F32)
        c_mur = mksb(es, "c_mur", [128, 8], F32)
        c_muk = mksb(es, "c_muk", [128, 8], F32)
        c_pm = mksb(es, "c_pm", [128, PCH], F32)
        c_m0 = mksb(es, "c_m0", [128, 1], F32)
        x1m1 = mksb(es, "x1m1", [128, 8], F32)
        xlast = mksb(es, "xlast", [128, 8 * NTILE], F32)

        es13 = ExitStack()
        pkm = mksb(es13, "pkm", [128, 128], BF16)
        xb = [mksb(es13, "xb0", [128, 8 * 512], BF16),
              mksb(es13, "xb1", [128, 8 * 512], BF16)]
        xstg = [mksb(es13, "xg0", [128, 512], F32),
                mksb(es13, "xg1", [128, 512], F32)]
        kexp = mksb(es13, "kexp", [128, 16 * 128], BF16)
        sexp = mksb(es13, "sexp", [128, 16 * 64], BF16)
        kfm = mksb(es13, "kfm", [128, 8 * T], BF16)
        vaug = mksb(es13, "vaug", [128, NCHUNK * H * 66], BF16)
        snaps = mksb(es13, "snaps", [128, NCHUNK * 528], BF16)
        sabs = mksb(es13, "sabs", [128, 8 * 65], F32)
        w_q = mksb(es13, "w_q", [128, 8 * 1024], BF16)
        w_out = mksb(es13, "w_out", [128, 8 * 1024], BF16)

        @blk.sync
        def _(sync):
            for src, dst in [(d_ident, c_id), (d_mask, c_mask), (d_e31, c_e31),
                             (d_ebc, c_ebc), (d_ones, c_ones), (d_outb, c_outb),
                             (d_mur, c_mur), (d_muk, c_muk), (d_pmask, c_pm),
                             (d_m0, c_m0)]:
                S.inc(sync.dma_start(out=dst[:, :], in_=src), "cl", 16)

        es1 = ExitStack()
        w_kv = mksb(es1, "w_kv", [128, 8 * 2048], BF16)
        c_brkv = mksb(es1, "c_brkv", [1, 2 * E], BF16)
        ktm = [mksb(es1, "ktm0", [128, E], BF16),
               mksb(es1, "ktm1", [128, E], BF16)]
        vtmp = [mksb(es1, "vtm0", [128, H * 66], BF16),
                mksb(es1, "vtm1", [128, H * 66], BF16)]
        _r1b = mksb(es1, "r1b0", [128, E], BF16)
        r1b = [_r1b, _r1b]
        _n1f = mksb(es1, "n1f0", [128, E], F32)
        n1f = [_n1f, _n1f]
        _ebb = mksb(es1, "eb0", [128, E], BF16)
        ebb = [_ebb, _ebb]

        @blk.gpsimd
        def _(g):
            S.inc(g.dma_start(out=c_brow[:, :], in_=d_brow[:, 0:E]), "swb", 16)
            S.inc(g.dma_start(out=c_brkv[:, :], in_=d_brow[:, E:]), "swb", 16)
            for c in range(8):
                S.inc(g.dma_start(out=w_kv[:, c * 2048:(c + 1) * 2048],
                                  in_=d_qkvw[c * 128:(c + 1) * 128, 1024:]),
                      "swkv", 16)
            for c in range(8):
                S.inc(g.dma_start(out=w_q[:, c * 1024:(c + 1) * 1024],
                                  in_=d_qkvw[c * 128:(c + 1) * 128, 0:1024]),
                      "swq", 16)
            for c in range(8):
                S.inc(g.dma_start(out=w_out[:, c * 1024:(c + 1) * 1024],
                                  in_=d_outw[c * 128:(c + 1) * 128, :]),
                      "swo", 16)

        @blk.vector
        def _(v):
            S.wait(v, "cl")
            va = vaug[:, :].rearrange("p (j s) -> p j s", s=66)
            S.inc(v.memset(va[:, :, 64:66], 0.0), "vc", 1)
            S.wait(v, "vc")
            S.inc(v.memset(va[:, :, 64:65], 1.0), "vc", 1)
            S.inc(v.memset(pkm[:, :], 0.0), "vc", 1)
            S.inc(v.memset(sabs[:, :], 0.0), "vc", 1)
            S.inc(v.memset(vtmp[0][:, :], 0.0), "vc", 1)
            S.inc(v.memset(vtmp[1][:, :], 0.0), "vc", 1)
            S.wait(v, "vc")
            for _vt in vtmp:
                vv = _vt[:, :].rearrange("p (h s) -> p h s", s=66)
                S.inc(v.memset(vv[:, :, 64:65], 1.0), "vc", 1)

        es1p = ExitStack()
        p_kv_k = es1p.enter_context(nc.psum_tensor("pkvk", [128, 1024], F32))
        p_kv_v = es1p.enter_context(nc.psum_tensor("pkvv", [128, 1024], F32))
        p_st = [es1p.enter_context(nc.psum_tensor(f"pst{i}", [128, 512], F32))
                for i in range(2)]
        p_tr = es1p.enter_context(nc.psum_tensor("ptr", [128, 1024], BF16))

        mk = dict(eluv=[], elua=[], stT=[], xc={}, pkfree=[], cmin=0)

        def emit_xload(tile, which, key):
            par = tile % 2
            src = d_xpfx if which == "pfx" else d_xsl
            for c in range(8):
                @blk.sync
                def _(sync, c=c):
                    S.wait(sync, "xc", max(0, S.val("xc") - 1))
                    S.inc(sync.dma_start(
                        out=xstg[c % 2][:, :],
                        in_=src[c * 128:(c + 1) * 128,
                                tile * 512:(tile + 1) * 512]),
                        "xl%d" % (c % 2), 16)

                @blk.gpsimd
                def _(g_, c=c):
                    S.wait(g_, "xl%d" % (c % 2))
                    S.wait(g_, "xbfree", S.val("xbfree"))
                    S.inc(g_.tensor_copy(xb[par][:, c * 512:(c + 1) * 512],
                                         xstg[c % 2][:, :]), "xc", 1)
            mk["xc"][key] = S.val("xc")

        def emit_kv_chunk(j, xtile_par, jj, masked_col, own_j, xc_key,
                          vbias=True):
            par = j % 2

            def xchunk(c):
                return xb[xtile_par][:, c * 512 + jj * 128:
                                     c * 512 + jj * 128 + 128]

            @blk.tensor
            def _(t):
                if jj == 0:
                    S.wait(t, "xc", mk["xc"][xc_key])
                if mk["pkfree"]:
                    S.wait(t, "eluv", mk["pkfree"][-1])
                if "proj" in pskip:
                    S.inc(t.nop(), "projK", 1)
                    S.inc(t.nop(), "projT", 2)
                    return
                for half in range(2):
                    for c in range(8):
                        t.matmul(p_kv_k[:, half * 512:(half + 1) * 512],
                                 xchunk(c),
                                 w_kv[:, c * 2048 + half * 512:
                                      c * 2048 + (half + 1) * 512],
                                 start=(c == 0), stop=False)
                    inst = t.matmul(p_kv_k[:, half * 512:(half + 1) * 512],
                             c_ones[0:1, 0:128],
                             c_brkv[:, half * 512:(half + 1) * 512],
                             start=False, stop=True)
                S.inc(inst, "projK", 1)
                if mk["eluv"]:
                    S.wait(t, "eluv", mk["eluv"][-1])
                for half in range(2):
                    for c in range(8):
                        inst = t.matmul(
                            p_kv_v[:, half * 512:(half + 1) * 512],
                            xchunk(c),
                            w_kv[:, c * 2048 + 1024 + half * 512:
                                 c * 2048 + 1024 + (half + 1) * 512],
                            start=(c == 0),
                            stop=(c == 7 and not vbias))
                    if vbias:
                        inst = t.matmul(p_kv_v[:, half * 512:(half + 1) * 512],
                                        c_ones[0:1, 0:128],
                                        c_brkv[:, 1024 + half * 512:
                                               1024 + (half + 1) * 512],
                                        start=False, stop=True)
                    S.inc(inst, "projT", 1)

            @blk.vector
            def _(v):
                S.wait(v, "projK")
                S.wait(v, "elua")
                if "elu" in pskip:
                    S.inc(v.nop(), "eluv", 2)
                    mk["cmin"] = S.val("eluv")
                    return
                S.inc(v.tensor_scalar_min(n1f[par][:, :], p_kv_k[:, :], 0.0),
                      "eluv", 1)
                mk["cmin"] = S.val("eluv")
                S.inc(v.tensor_scalar_max(r1b[par][:, :], p_kv_k[:, :], 0.0),
                      "eluv", 1)

            mk["pkfree"].append(S.val("eluv"))

            @blk.scalar
            def _(sc):
                S.wait(sc, "eluv", mk["cmin"])
                if "act" in pskip:
                    S.inc(sc.nop(), "elua", 1)
                    return
                i1 = sc.activation(ebb[par][:, :], n1f[par][:, :], AF.Exp,
                                   bias=0.0, scale=1.0)
                S.inc(i1, "elua", 1)

            @blk.vector
            def _(v):
                S.wait(v, "elua")
                S.wait(v, "eluv", mk["pkfree"][-1])
                if mk["stT"]:
                    S.wait(v, "stT", mk["stT"][-1])
                if "elu" in pskip:
                    for _i in range(4 if masked_col is not None else 3):
                        S.inc(v.nop(), "eluv", 1)
                    return
                S.inc(v.tensor_add(ktm[par][:, :], r1b[par][:, :],
                                   ebb[par][:, :]), "eluv", 1)
                if masked_col is not None:
                    S.wait(v, "eluv")
                    S.inc(v.tensor_scalar_mul(
                        ktm[par][:, :], ktm[par][:, :],
                        c_pm[:, masked_col:masked_col + 1]), "eluv", 1)
                if own_j is None:
                    vdst = vtmp[par][:, :].rearrange("p (h s) -> p h s", s=66)
                else:
                    vdst = vaug[:, own_j * H * 66:(own_j + 1) * H * 66
                                ].rearrange("p (h s) -> p h s", s=66)
                for half in range(2):
                    S.wait(v, "projT", S.val("projT") - (1 - half))
                    S.inc(v.tensor_copy(
                        vdst[:, half * 8:(half + 1) * 8, 0:64],
                        p_kv_v[:, half * 512:(half + 1) * 512].rearrange(
                            "p (h s) -> p h s", s=64)), "eluv", 1)

            mk["eluv"].append(S.val("eluv"))
            mk["elua"].append(S.val("elua"))
            eluv_end = S.val("eluv")

            def do_state():
                if own_j is not None:
                    @blk.tensor
                    def _(t):
                        S.wait(t, "eluv", eluv_end)
                        S.wait(t, "ktr", S.val("ktr"))
                        for c in range(8):
                            inst = t.transpose(
                                p_tr[:, c * 128:(c + 1) * 128],
                                ktm[par][:, c * 128:(c + 1) * 128],
                                c_id[:, :])
                        S.inc(inst, "trT", 1)

                    @blk.vector
                    def _(v):
                        S.wait(v, "trT")
                        kdst = kfm[:, :].rearrange("p (c t) -> p c t", t=T)
                        S.inc(v.tensor_copy(
                            kdst[:, :, own_j * 128:(own_j + 1) * 128],
                            p_tr[:, :].rearrange("p (c t) -> p c t", t=128)),
                            "ktr", 1)

                @blk.tensor
                def _(t):
                    S.wait(t, "eluv", eluv_end)
                    if "state" in pskip:
                        S.inc(t.nop(), "stT", 1)
                        return
                    if own_j is not None:
                        S.wait(t, "snapv", S.val("snapv"))
                    vsrc = (vtmp[par] if own_j is None else
                            vaug[:, own_j * H * 66:(own_j + 1) * H * 66])
                    for h in range(H):
                        bank = p_st[h // 8]
                        pair_col = ((h % 8) // 2) * 65
                        half = h % 2
                        if own_j is None:
                            st = (j == 0 and h in (0, 1, 8, 9))
                            sp = False
                        else:
                            st = h in (0, 1, 8, 9)
                            sp = h in (6, 7, 14, 15)
                        inst = t.matmul(
                            bank[half * 64:half * 64 + 64,
                                 pair_col:pair_col + 65],
                            ktm[par][:, h * 64:(h + 1) * 64],
                            vsrc[:, h * 66:h * 66 + 65],
                            start=st, stop=sp, skip_group_check=True)
                    S.inc(inst, "stT", 1)

                mk["stT"].append(S.val("stT"))

                if own_j is not None:
                    @blk.vector
                    def _(v):
                        S.wait(v, "stT")
                        sdst = snaps[:, own_j * 528:(own_j + 1) * 528]
                        S.inc(v.tensor_copy(
                            sdst.rearrange("p (c s) -> p c s",
                                           s=66)[:, :, 0:65],
                            sabs[:, :].rearrange("p (c s) -> p c s", s=65)),
                            "snapv", 1)
                        S.wait(v, "snapv")
                        for b in range(2):
                            inst = v.tensor_add(
                                sabs[:, b * 260:(b + 1) * 260],
                                sabs[:, b * 260:(b + 1) * 260],
                                p_st[b][:, 0:260])
                            S.inc(inst, "snapv", 1)
                        S.wait(v, "snapv")

            return do_state

        @blk.tensor
        def _(t):
            S.wait(t, "swkv")
            S.wait(t, "swb")
            S.wait(t, "cl")

        @blk.vector
        def _(v):
            S.wait(v, "cl")
            S.wait(v, "vc")

        emit_xload(0, "pfx", ("pfx", 0))
        pend = None
        for j in range(PCH):
            if j % 4 == 0 and j // 4 + 1 < NPT:
                emit_xload(j // 4 + 1, "pfx", ("pfx", j // 4 + 1))
            nxt = emit_kv_chunk(j, (j // 4) % 2, j % 4, masked_col=j,
                                own_j=None, xc_key=("pfx", j // 4),
                                vbias=False)
            if pend is not None:
                pend()
            pend = nxt
            if j % 4 == 3:
                @blk.tensor
                def _(t):
                    S.inc(t.nop(), "xbfree", 1)
        pend()

        rowT2 = mksb(es1, "rowT2", [1, H * 64], BF16)
        scc = mksb(es1, "scc", [128, 8], F32)

        @blk.vector
        def _(v):
            S.wait(v, "stT")
            for b in range(2):
                S.inc(v.tensor_copy(scc[:, b * 4:(b + 1) * 4],
                                    p_st[b][:, 64:260:65]), "vbc", 1)

        @blk.gpsimd
        def _(g):
            S.wait(g, "vbc")
            for h in range(H):
                b, half = h // 8, h % 2
                pr = (h % 8) // 2
                S.inc(g.dma_start(
                    out=rowT2[0:1, h * 64:(h + 1) * 64],
                    in_=scc[half * 64:half * 64 + 64, b * 4 + pr:
                            b * 4 + pr + 1]),
                    "vbf", 16)

        @blk.tensor
        def _(t):
            S.wait(t, "vbf")
            for h in range(H):
                b, half = h // 8, h % 2
                pair_col = ((h % 8) // 2) * 65
                inst = t.matmul(
                    p_st[b][half * 64:half * 64 + 64,
                            pair_col:pair_col + 64],
                    rowT2[0:1, h * 64:(h + 1) * 64],
                    c_brkv[0:1, 1024 + h * 64:1024 + (h + 1) * 64],
                    start=False, stop=(h in (6, 7, 14, 15)),
                    skip_group_check=True)
            S.inc(inst, "stT", 1)

        @blk.vector
        def _(v):
            S.wait(v, "stT")
            for b in range(2):
                S.inc(v.tensor_copy(sabs[:, b * 260:(b + 1) * 260],
                                    p_st[b][:, 0:260]), "snapv", 1)
            S.wait(v, "snapv")

        pend = None
        for j in range(NCHUNK):
            if j % 4 == 0:
                emit_xload(j // 4, "own", ("own", j // 4))
            nxt = emit_kv_chunk(j, (j // 4) % 2, j % 4, masked_col=None,
                                own_j=j, xc_key=("own", j // 4))
            if pend is not None:
                pend()
            pend = nxt
            if j % 4 == 3:
                @blk.tensor
                def _(t):
                    S.inc(t.nop(), "xbfree", 1)
        pend()

        @blk.tensor
        def _(t):
            S.wait(t, "eluv", S.val("eluv"))
            S.wait(t, "elua", S.val("elua"))
            S.wait(t, "snapv", S.val("snapv"))
            S.wait(t, "ktr", S.val("ktr"))
            S.inc(t.nop(), "p1done", 1)

        es1p.close()
        es1.close()

        if phases < 2:
            raise _PhaseCut
        es2 = ExitStack()
        p_a8 = es2.enter_context(nc.psum_tensor("pm1a", [128, 512], F32))
        p_b8 = es2.enter_context(nc.psum_tensor("pm1b", [128, 512], F32))
        p_n8 = es2.enter_context(nc.psum_tensor("pm1n", [16, 512], F32))

        m1_x = mksb(es2, "m1x", [128, 8], F32)
        m1_xb = mksb(es2, "m1xb", [128, 8], BF16)
        m1_q = mksb(es2, "m1q", [128, 8], BF16)
        m1_r1 = mksb(es2, "m1r1", [128, 8], BF16)
        m1_n1 = mksb(es2, "m1n1", [128, 8], F32)
        m1_e = mksb(es2, "m1e", [128, 8], BF16)
        m1_rn = mksb(es2, "m1rn", [16, 2], F32)
        m1_rnb = mksb(es2, "m1rnb", [16, 1], BF16)
        m1_rbc = mksb(es2, "m1rbc", [128, 8], BF16)
        m1_ao = mksb(es2, "m1ao", [128, 8], BF16)
        m1_a = mksb(es2, "m1a", [128, 8], F32)
        m1_ab = mksb(es2, "m1ab", [128, 8], BF16)
        m1_sq = mksb(es2, "m1sq", [128, 8], BF16)
        m1_s1 = mksb(es2, "m1s1", [128, 1], F32)
        m1_s2 = mksb(es2, "m1s2", [128, 1], F32)
        m1_t0 = mksb(es2, "m1t0", [128, 1], F32)
        m1_t1 = mksb(es2, "m1t1", [128, 1], F32)
        m1_se = mksb(es2, "m1se", [128, 16 * 64], BF16)

        @blk.sync
        def _(sync):
            S.wait(sync, "p1done")
            S.inc(sync.dma_start(out=m1_x[:, :], in_=d_xm1), "m1l", 16)

        @blk.vector
        def _(v):
            S.wait(v, "m1l")
            S.inc(v.tensor_copy(m1_xb[:, :], m1_x[:, :]), "m1v", 1)

        @blk.tensor
        def _(t):
            S.wait(t, "p1done")
            S.wait(t, "swq")
            S.wait(t, "m1v")
            for f in range(8):
                for c in range(8):
                    t.matmul(p_a8[:, f:f + 1],
                             w_q[:, c * 1024 + f * 128:
                                 c * 1024 + (f + 1) * 128],
                             m1_xb[:, c:c + 1],
                             start=(c == 0), stop=False)
                inst = t.matmul(p_a8[:, f:f + 1],
                                c_brow[0:1, f * 128:(f + 1) * 128],
                                c_ones[0:1, 0:1],
                                start=False, stop=True)
            S.inc(inst, "m1T", 1)

        @blk.vector
        def _(v):
            S.wait(v, "m1T")
            S.inc(v.tensor_scalar_min(m1_n1[:, :], p_a8[:, 0:8], 0.0),
                  "m1v", 1)
            mk["m1cmin"] = S.val("m1v")
            S.inc(v.tensor_scalar_max(m1_r1[:, :], p_a8[:, 0:8], 0.0), "m1v", 1)
            mk["m1max"] = S.val("m1v")

        @blk.scalar
        def _(sc):
            S.wait(sc, "m1v", mk["m1cmin"])
            i1 = sc.activation(m1_e[:, :], m1_n1[:, :], AF.Exp,
                               bias=0.0, scale=1.0)
            S.inc(i1, "m1a", 1)

        @blk.vector
        def _(v):
            S.wait(v, "m1a")
            S.wait(v, "m1v", mk["m1max"])
            S.inc(v.tensor_add(m1_q[:, :], m1_r1[:, :], m1_e[:, :]), "m1v", 1)
            S.inc(v.memset(m1_se[:, :], 0.0), "m1v", 1)
            S.wait(v, "m1v")
            se = m1_se[:, :].rearrange("p (h d) -> p h d", d=64)
            sn = snaps[:, 0:528].rearrange("p (c s) -> p c s", s=66)
            for half in range(2):
                sl = slice(half * 64, half * 64 + 64)
                S.inc(v.tensor_copy(se[sl, half::2, :], sn[sl, :, 0:64]),
                      "m1v", 1)
            for half in range(2):
                src = snaps[half * 64:(half + 1) * 64,
                            64:64 + 7 * 66 + 1:66]
                dst = pkm[half * 64:(half + 1) * 64,
                          half:half + 7 * 18 + 1:18]
                inst = v.tensor_copy(dst, src)
            S.inc(inst, "m1v", 1)

        @blk.tensor
        def _(t):
            S.wait(t, "m1v")
            S.wait(t, "m1a")
            for h in range(H):
                c, half = h // 2, h % 2
                t.matmul(p_b8[half * 64:half * 64 + 64, c:c + 1],
                         m1_se[:, h * 64:(h + 1) * 64],
                         m1_q[:, c:c + 1],
                         start=(h in (0, 1)), stop=(h in (14, 15)),
                         skip_group_check=True)
            for c in range(8):
                inst = t.matmul(p_n8[:, c:c + 1], pkm[:, 16 * c:16 * c + 16],
                                m1_q[:, c:c + 1],
                                start=(c == 0), stop=(c == 7))
            S.inc(inst, "m1T", 1)

        @blk.vector
        def _(v):
            S.wait(v, "m1T")
            S.inc(v.reduce_sum(m1_rn[:, 0:1], p_n8[:, 0:8], axis=AX.X), "m1v", 1)
            S.wait(v, "m1v")
            S.inc(v.tensor_scalar_add(m1_rn[:, 0:1], m1_rn[:, 0:1], 1e-6),
                  "m1v", 1)
            S.wait(v, "m1v")
            S.inc(v.reciprocal(m1_rn[:, 1:2], m1_rn[:, 0:1]), "m1v", 1)
            S.wait(v, "m1v")
            S.inc(v.tensor_copy(m1_rnb[:, :], m1_rn[:, 1:2]), "m1v", 1)

        @blk.tensor
        def _(t):
            S.wait(t, "m1v")
            for c in range(8):
                inst = t.matmul(p_a8[:, c:c + 1],
                                c_ebc[:, c * 128:(c + 1) * 128],
                                m1_rnb[:, :],
                                start=(c == 0), stop=(c == 7))
            S.inc(inst, "m1T", 1)

        @blk.vector
        def _(v):
            S.wait(v, "m1T")
            S.inc(v.tensor_copy(m1_rbc[:, :], p_a8[:, 0:8]), "m1v", 1)
            S.wait(v, "m1v")
            S.inc(v.tensor_tensor(m1_ao[:, :], p_b8[:, 0:8], m1_rbc[:, :],
                                  ALU.mult), "m1v", 1)

        @blk.tensor
        def _(t):
            S.wait(t, "m1v")
            S.wait(t, "swo")
            for f in range(8):
                for c in range(8):
                    inst = t.matmul(p_b8[:, f:f + 1],
                                    w_out[:, c * 1024 + f * 128:
                                          c * 1024 + (f + 1) * 128],
                                    m1_ao[:, c:c + 1],
                                    start=(c == 0), stop=(c == 7))
            S.inc(inst, "m1T", 1)

        @blk.vector
        def _(v):
            S.wait(v, "m1T")
            S.inc(v.scalar_tensor_tensor(m1_a[:, :], p_b8[:, 0:8], 1.0,
                                         m1_x[:, :], ALU.mult, ALU.add),
                  "m1v", 1)
            S.wait(v, "m1v")
            for c in range(8):
                inst = v.tensor_scalar_add(m1_a[:, c:c + 1], m1_a[:, c:c + 1],
                                           c_outb[:, c:c + 1])
            S.inc(inst, "m1v", 1)
            S.wait(v, "m1v")
            S.inc(v.tensor_copy(m1_ab[:, :], m1_a[:, :]), "m1v", 1)
            S.wait(v, "m1v")
            S.inc(v.tensor_mul(m1_sq[:, :], m1_ab[:, :], m1_ab[:, :]),
                  "m1v", 1)

        @blk.tensor
        def _(t):
            S.wait(t, "m1v")
            for c in range(8):
                t.matmul(p_a8[:, 0:1], c_ones[:, 0:128], m1_ab[:, c:c + 1],
                         start=(c == 0), stop=(c == 7))
            for c in range(8):
                inst = t.matmul(p_a8[:, 1:2], c_ones[:, 0:128],
                                m1_sq[:, c:c + 1],
                                start=(c == 0), stop=(c == 7))
            S.inc(inst, "m1T", 1)

        @blk.vector
        def _(v):
            S.wait(v, "m1T")
            S.inc(v.tensor_scalar_mul(m1_s1[:, :], p_a8[:, 0:1], 1.0 / E),
                  "m1v", 1)
            S.inc(v.tensor_scalar_mul(m1_s2[:, :], p_a8[:, 1:2], 1.0 / E),
                  "m1v", 1)
            S.wait(v, "m1v")
            S.inc(v.tensor_mul(m1_t0[:, :], m1_s1[:, :], m1_s1[:, :]),
                  "m1v", 1)
            S.wait(v, "m1v")
            S.inc(v.scalar_tensor_tensor(m1_t1[:, :], m1_s2[:, :], 1e-5,
                                         m1_t0[:, :], ALU.add,
                                         ALU.subtract), "m1v", 1)
            S.wait(v, "m1v")

        @blk.scalar
        def _(sc):
            S.wait(sc, "m1v")
            S.inc(sc.activation(m1_t1[:, :], m1_t1[:, :], AF.Sqrt,
                                bias=0.0, scale=1.0), "m1a", 1)

        @blk.vector
        def _(v):
            S.wait(v, "m1a")
            S.inc(v.reciprocal(m1_t0[:, :], m1_t1[:, :]), "m1v", 1)
            S.wait(v, "m1v")
            S.inc(v.tensor_scalar(x1m1[:, :], m1_a[:, :], m1_s1[:, 0:1],
                                  None, ALU.subtract), "m1v", 1)
            S.wait(v, "m1v")
            S.inc(v.tensor_scalar_mul(x1m1[:, :], x1m1[:, :], m1_t0[:, 0:1]),
                  "m1v", 1)
            S.wait(v, "m1v")
            S.inc(v.tensor_scalar_mul(x1m1[:, :], x1m1[:, :], c_m0[:, 0:1]),
                  "m1v", 1)
            S.wait(v, "m1v")

        @blk.tensor
        def _(t):
            S.wait(t, "m1v")
            S.wait(t, "m1a")
            S.inc(t.nop(), "p2done", 1)

        es2.close()

        if phases < 3:
            raise _PhaseCut
        es3 = ExitStack()
        p_A = es3.enter_context(nc.psum_tensor("pA", [128, 1024], F32))
        p_C = es3.enter_context(nc.psum_tensor("pC", [128, 1024], F32))
        p_N = es3.enter_context(nc.psum_tensor("pN", [16, 512], F32))
        p_R = es3.enter_context(nc.psum_tensor("pR", [128, 1024], F32))
        qp = mksb(es3, "qp", [128, 8 * 512], BF16)
        qn1 = [mksb(es3, "qn1a", [128, 512], F32),
               mksb(es3, "qn1b", [128, 512], F32)]
        qr1 = [mksb(es3, "qr1a", [128, 512], BF16),
               mksb(es3, "qr1b", [128, 512], BF16)]
        qe = [mksb(es3, "qea", [128, 512], BF16),
              mksb(es3, "qeb", [128, 512], BF16)]
        sbT = mksb(es3, "sbTt", [128, 2048], BF16)
        rn16 = mksb(es3, "rn16", [16, 128], F32)
        rn16b = mksb(es3, "rn16b", [16, 128], BF16)
        rnbc = mksb(es3, "rnbc", [128, 1024], BF16)
        attnb = mksb(es3, "attnb", [128, 8 * 512], BF16)
        ab = [mksb(es3, "ab0", [128, 512], BF16),
              mksb(es3, "ab1", [128, 512], BF16)]
        sqb = [mksb(es3, "sqb0", [128, 512], BF16),
               mksb(es3, "sqb1", [128, 512], BF16)]
        msb = mksb(es3, "msb", [128, 512], F32)
        vsb = mksb(es3, "vsb", [128, 512], F32)
        s2sb = qn1[0]
        af = mksb(es3, "af", [128, 8 * 512], F32)
        xres = [mksb(es3, "xr0", [128, 512], F32),
                mksb(es3, "xr1", [128, 512], F32)]
        x1c = [qn1[1], qn1[1]]

        @blk.vector
        def _(v):
            S.wait(v, "m1v")
            S.wait(v, "m1a")
            S.inc(v.memset(kexp[:, :], 0.0), "kexpz", 1)
            S.inc(v.memset(sexp[:, :], 0.0), "kexpz", 1)

        for tile in range(NTILE):
            emit_xload(tile, "own", ("main", tile))

            for fp in range(4):
                for fo in range(2):
                    f = 2 * fp + fo
                    bs = (p_A, p_A, p_C, p_C)[f % 4][
                        :, (f % 2) * 512:(f % 2) * 512 + 512]
                    mk[("qbs", f % 2)] = bs

                    @blk.tensor
                    def _(t, f=f, bs=bs, tile=tile):
                        S.wait(t, "xc", mk["xc"][("main", tile)])
                        if tile == 0 and f == 0:
                            S.wait(t, "p2done")
                        if f < 4:
                            S.wait(t, "resv", S.val("resv"))
                            S.wait(t, "lnv", S.val("lnv"))
                            S.wait(t, "lapp", S.val("lapp"))
                            S.wait(t, "x1st0", S.val("x1st0"))
                            S.wait(t, "x1st1", S.val("x1st1"))
                        else:
                            S.wait(t, "qelu", mk[("qmax", tile, f - 4)])
                        par = tile % 2
                        for c in range(8):
                            t.matmul(bs,
                                     w_q[:, c * 1024 + f * 128:
                                         c * 1024 + (f + 1) * 128],
                                     xb[par][:, c * 512:(c + 1) * 512],
                                     start=(c == 0), stop=False)
                        inst = t.matmul(bs,
                                        c_brow[0:1, f * 128:(f + 1) * 128],
                                        c_ones[0:1, 0:512],
                                        start=False, stop=True)
                        S.inc(inst, "qprojT", 1)

                    @blk.vector
                    def _(v, f=f, fo=fo, bs=bs, tile=tile):
                        S.wait(v, "qprojT")
                        S.wait(v, "qelua")
                        S.inc(v.tensor_scalar_min(qn1[fo][:, :], bs, 0.0),
                              "qelu", 1)
                        mk[("qcmin", tile, f)] = S.val("qelu")
                        S.inc(v.tensor_scalar_max(qr1[fo][:, :], bs, 0.0),
                              "qelu", 1)
                        mk[("qmax", tile, f)] = S.val("qelu")

                @blk.scalar
                def _(sc, fp=fp, tile=tile):
                    for fo in range(2):
                        S.wait(sc, "qelu", mk[("qcmin", tile, 2 * fp + fo)])
                        S.inc(sc.activation(qe[fo][:, :], qn1[fo][:, :],
                                            AF.Exp, bias=0.0, scale=1.0),
                              "qelua", 1)

                for fo in range(2):
                    @blk.vector
                    def _(v, f=2 * fp + fo, fo=fo):
                        S.wait(v, "qelua", S.val("qelua") - (1 - fo))
                        S.wait(v, "qelu", mk[("qmax", tile, 2 * fp + fo)])
                        S.inc(v.tensor_add(qp[:, f * 512:(f + 1) * 512],
                                           qr1[fo][:, :], qe[fo][:, :]),
                              "qelu", 1)

            @blk.tensor
            def _(t):
                S.inc(t.nop(), "xbfree", 1)

            for jj in range(4 if p3sub >= 2 else 0):
                jglob = tile * 4 + jj

                @blk.vector
                def _(v, jglob=jglob):
                    S.wait(v, "qelu")
                    S.wait(v, "normT", S.val("normT"))
                    if jglob == 0:
                        S.wait(v, "kexpz")
                    S.wait(v, "scoreT", S.val("scoreT"))
                    S.wait(v, "svT", S.val("svT"))
                    ke = kexp[:, :].rearrange("p (h t) -> p h t", t=128)
                    kf = kfm[:, :].rearrange("p (c t) -> p c t", t=T)
                    se = sexp[:, :].rearrange("p (h d) -> p h d", d=64)
                    sn = snaps[:, jglob * 528:(jglob + 1) * 528].rearrange(
                        "p (c s) -> p c s", s=66)
                    for half in range(2):
                        sl = slice(half * 64, half * 64 + 64)
                        S.inc(v.tensor_copy(
                            ke[sl, half::2, :],
                            kf[sl, :, jglob * 128:(jglob + 1) * 128]),
                            "pkmv", 1)
                        S.inc(v.tensor_copy(se[sl, half::2, :],
                                            sn[sl, :, 0:64]), "pkmv", 1)
                    if p3sub != 22:
                        for half in range(2):
                            src = snaps[half * 64:(half + 1) * 64,
                                        jglob * 528 + 64:
                                        jglob * 528 + 64 + 7 * 66 + 1:66]
                            dst = pkm[half * 64:(half + 1) * 64,
                                      half:half + 7 * 18 + 1:18]
                            inst = v.tensor_copy(dst, src)
                    else:
                        inst = v.memset(pkm[:, 0:1], 0.0)
                    S.inc(inst, "pkmv", 1)

                for wave in range(4 if p3sub != 21 else 0):
                    bnk = (wave % 2) * 512

                    @blk.tensor
                    def _(t, jglob=jglob, jj=jj, wave=wave, bnk=bnk):
                        if wave == 0:
                            S.wait(t, "qelu")
                            S.wait(t, "pkmv")
                        S.wait(t, "maskv", max(0, S.val("maskv") - 1))
                        for i in range(4):
                            h = wave * 4 + i
                            c = h // 2
                            inst = t.matmul(
                                p_A[:, bnk + i * 128:bnk + (i + 1) * 128],
                                kexp[:, h * 128:(h + 1) * 128],
                                qp[:, c * 512 + jj * 128:
                                   c * 512 + jj * 128 + 128],
                                start=(i == 0), stop=(i == 3))
                        S.inc(inst, "scoreT", 1)

                    @blk.vector
                    def _(v, wave=wave, bnk=bnk):
                        S.wait(v, "scoreT")
                        S.wait(v, "svT", S.val("svT"))
                        S.wait(v, "normT", S.val("normT"))
                        if p3sub == 23:
                            inst = v.memset(sbT[:, 0:1], 0.0)
                        else:
                            inst = v.tensor_tensor(
                                sbT[:, wave * 512:(wave + 1) * 512],
                                p_A[:, bnk:bnk + 512], c_mask[:, 0:512],
                                ALU.mult)
                        S.inc(inst, "maskv", 1)

                if p3sub < 3:
                    continue

                if mk.get("pendrn") is not None:
                    mk["pendrn"]()
                    mk["pendrn"] = None
                if mk.get("pend3") is not None:
                    mk["pend3"]()
                    mk["pend3"] = None

                @blk.tensor
                def _(t, jglob=jglob, jj=jj):
                    S.wait(t, "pkmv")
                    S.wait(t, "aov", S.val("aov"))
                    S.wait(t, "lnv", S.val("lnv"))
                    for h in range(H):
                        c, half = h // 2, h % 2
                        t.matmul(
                            p_C[half * 64:half * 64 + 64,
                                c * 128:(c + 1) * 128],
                            sexp[:, h * 64:(h + 1) * 64],
                            qp[:, c * 512 + jj * 128:c * 512 + jj * 128 + 128],
                            start=(h in (0, 1, 8, 9)), stop=False,
                            skip_group_check=True)
                    S.wait(t, "rn16v", S.val("rn16v"))
                    for c in range(8):
                        t.matmul(p_N[:, 0:128],
                                 pkm[:, 16 * c:16 * c + 16],
                                 qp[:, c * 512 + jj * 128:
                                    c * 512 + jj * 128 + 128],
                                 start=(c == 0), stop=False)
                    S.wait(t, "maskv")
                    for h in list(range(0, H, 2)) + list(range(1, H, 2)):
                        c, half = h // 2, h % 2
                        inst = t.matmul(p_C[half * 64:half * 64 + 64,
                                            c * 128:(c + 1) * 128],
                                        vaug[:, jglob * H * 66 + h * 66:
                                             jglob * H * 66 + h * 66 + 64],
                                        sbT[:, h * 128:(h + 1) * 128],
                                        start=False,
                                        stop=(h in (6, 7, 14, 15)),
                                        skip_group_check=True)
                    S.inc(inst, "svT", 1)
                    for h in range(H):
                        inst = t.matmul(p_N[:, 0:128],
                                        c_e31[:, 15 - h:31 - h],
                                        sbT[:, h * 128:(h + 1) * 128],
                                        start=False, stop=(h == 15))
                    S.inc(inst, "normT", 1)

                def rnchain3():
                    @blk.vector
                    def _(v):
                        S.wait(v, "normT")
                        S.wait(v, "rnT", S.val("rnT"))
                        S.inc(v.tensor_scalar_add(rn16[:, :], p_N[:, 0:128],
                                                  1e-6), "rn16v", 1)
                        S.wait(v, "rn16v")
                        S.inc(v.reciprocal(rn16[:, :], rn16[:, :]),
                              "rn16v", 1)
                        S.wait(v, "rn16v")
                        S.inc(v.tensor_copy(rn16b[:, :], rn16[:, :]),
                              "rn16v", 1)

                def tail3(jj=jj):
                    @blk.tensor
                    def _(t):
                        S.wait(t, "rn16v")
                        S.wait(t, "aov", S.val("aov"))
                        for c in range(8):
                            inst = t.matmul(p_R[:, c * 128:(c + 1) * 128],
                                            c_ebc[:, c * 128:(c + 1) * 128],
                                            rn16b[:, :],
                                            start=(c in (0, 4)),
                                            stop=(c in (3, 7)))
                        S.inc(inst, "rnT", 1)

                    @blk.vector
                    def _(v):
                        S.wait(v, "rnT")
                        S.wait(v, "svT")
                        S.wait(v, "oprojT", S.val("oprojT"))
                        adst = attnb[:, :].rearrange("p (c t) -> p c t",
                                                     t=512)
                        for b in range(2):
                            sl = slice(b * 512, (b + 1) * 512)
                            S.inc(v.tensor_copy(rnbc[:, sl], p_R[:, sl]),
                                  "rnc", 1)
                            S.wait(v, "rnc")
                            inst = v.tensor_tensor(
                                adst[:, b * 4:(b + 1) * 4,
                                     jj * 128:(jj + 1) * 128],
                                p_C[:, sl].rearrange("p (c t) -> p c t",
                                                     t=128),
                                rnbc[:, sl].rearrange("p (c t) -> p c t",
                                                      t=128),
                                ALU.mult)
                            S.inc(inst, "aov", 1)

                mk["pendrn"] = rnchain3
                mk["pend3"] = tail3

            if mk.get("pendrn") is not None:
                mk["pendrn"]()
                mk["pendrn"] = None
            if mk.get("pend3") is not None:
                mk["pend3"]()
                mk["pend3"] = None

            for f in range(8 if p3sub >= 4 else 0):
                bs = (p_A, p_A, p_C, p_C)[f % 4][
                    :, (f % 2) * 512:(f % 2) * 512 + 512]

                @blk.sync
                def _(sync, f=f, tile=tile):
                    if tile == 0 and f == 0:
                        S.wait(sync, "p2done")
                    if f >= 2:
                        S.wait(sync, "resv", S.val("resv") - 1)
                    S.inc(sync.dma_start(
                        out=xres[f % 2][:, :],
                        in_=d_xsl[f * 128:(f + 1) * 128,
                                  tile * 512:(tile + 1) * 512]),
                        "xrl%d" % (f % 2), 16)

                @blk.tensor
                def _(t, f=f, bs=bs, tile=tile):
                    S.wait(t, "aov")
                    if f < 2:
                        S.wait(t, "maskv", S.val("maskv"))
                    elif f >= 4:
                        S.wait(t, "resv", mk[("resv", tile, f - 4)])
                    for c in range(8):
                        inst = t.matmul(bs,
                                        w_out[:, c * 1024 + f * 128:
                                              c * 1024 + (f + 1) * 128],
                                        attnb[:, c * 512:(c + 1) * 512],
                                        start=(c == 0), stop=(c == 7))
                    S.inc(inst, "oprojT", 1)

                @blk.vector
                def _(v, f=f, bs=bs):
                    S.wait(v, "oprojT")
                    S.wait(v, "xrl%d" % (f % 2))
                    i0 = v.scalar_tensor_tensor(af[:, f * 512:(f + 1) * 512],
                                                bs, c_outb[:, f:f + 1],
                                                xres[f % 2][:, :],
                                                ALU.add, ALU.add)
                    S.inc(i0, "resv", 1)
                    mk[("resv", tile, f)] = S.val("resv")

            for c in range(8 if p3sub >= 5 else 0):
                @blk.vector
                def _(v, c=c):
                    if c < 2:
                        S.wait(v, "resv")
                    if c >= 2:
                        S.wait(v, "statT", S.val("statT"))
                    S.inc(v.tensor_copy(ab[c % 2][:, :],
                                        af[:, c * 512:(c + 1) * 512]),
                          "abv", 1)

                @blk.gpsimd
                def _(g, c=c):
                    S.wait(g, "abv")
                    S.inc(g.tensor_mul(sqb[c % 2][:, :], ab[c % 2][:, :],
                                       ab[c % 2][:, :]), "sqp", 1)

                @blk.tensor
                def _(t, c=c):
                    S.wait(t, "abv")
                    if c == 0:
                        S.wait(t, "lnv", S.val("lnv"))
                    t.matmul(p_C[:, 0:512], c_ones[:, 0:128], ab[c % 2][:, :],
                             start=(c == 0), stop=(c == 7))
                    S.wait(t, "sqp")
                    inst = t.matmul(p_C[:, 512:1024], c_ones[:, 0:128],
                                    sqb[c % 2][:, :],
                                    start=(c == 0), stop=(c == 7))
                    S.inc(inst, "statT", 1)

            if p3sub < 5:
                continue

            @blk.vector
            def _(v):
                S.wait(v, "statT")
                S.inc(v.tensor_scalar_mul(msb[:, :], p_C[:, 0:512], 1.0 / E),
                      "lnv", 1)
                S.inc(v.tensor_scalar_mul(s2sb[:, :], p_C[:, 512:1024],
                                          1.0 / E), "lnv", 1)
                S.wait(v, "lnv")
                S.inc(v.tensor_mul(vsb[:, :], msb[:, :], msb[:, :]), "lnv", 1)
                S.wait(v, "lnv")
                S.inc(v.scalar_tensor_tensor(vsb[:, :], s2sb[:, :], 1e-5,
                                             vsb[:, :], ALU.add,
                                             ALU.subtract), "lnv", 1)
                S.wait(v, "lnv")

            @blk.scalar
            def _(sc):
                S.wait(sc, "lnv")
                S.inc(sc.activation(s2sb[:, :], vsb[:, :], AF.Sqrt,
                                    bias=0.0, scale=1.0), "lna", 1)

            @blk.vector
            def _(v):
                S.wait(v, "lna")
                S.inc(v.reciprocal(vsb[:, :], s2sb[:, :]), "lnv", 1)
                S.wait(v, "lnv")

            for c in range(8):
                @blk.vector
                def _(v, c=c, tile=tile):
                    S.wait(v, "lnv")
                    S.wait(v, "x1st0")
                    S.wait(v, "x1st1")
                    S.inc(v.tensor_sub(x1c[c % 2][:, :],
                                       af[:, c * 512:(c + 1) * 512],
                                       msb[:, :]), "lapp", 1)

                @blk.gpsimd
                def _(g, c=c):
                    S.wait(g, "lapp")
                    S.inc(g.tensor_mul(x1c[c % 2][:, :], x1c[c % 2][:, :],
                                       vsb[:, :]), "lappp", 1)

                @blk.vector
                def _(v, c=c, tile=tile):
                    S.wait(v, "lappp")
                    S.inc(v.tensor_copy(xlast[:, tile * 8 + c:
                                              tile * 8 + c + 1],
                                        x1c[c % 2][:, 511:512]), "x1v", 1)

                @blk.sync
                def _(sync, c=c, tile=tile):
                    S.wait(sync, "x1v")
                    S.inc(sync.dma_start(
                        out=d_x1[c, :, tile * 512:(tile + 1) * 512],
                        in_=x1c[c % 2][:, :]), "x1st%d" % (c % 2), 16)

        @blk.tensor
        def _(t):
            S.wait(t, "lnv", S.val("lnv"))
            S.wait(t, "lappp", S.val("lappp"))
            S.wait(t, "rn16v", S.val("rn16v"))
            S.wait(t, "rnbcv", S.val("rnbcv"))
            S.wait(t, "aov", S.val("aov"))
            S.wait(t, "maskv", S.val("maskv"))
            S.wait(t, "resv", S.val("resv"))
            S.inc(t.nop(), "p3done", 1)

        es3.close()
        es13.close()

        if phases < 4:
            raise _PhaseCut
        es4 = ExitStack()
        p_F = es4.enter_context(nc.psum_tensor("pF", [128, 1024], F32))
        p_G = es4.enter_context(nc.psum_tensor("pG", [128, 1024], F32))
        w_ffn = mksb(es4, "w_ffn", [128, 3 * 8 * 1024], BF16)
        x1g = [mksb(es4, "x1g0", [128, 8 * 512], F32),
               mksb(es4, "x1g1", [128, 8 * 512], F32)]
        x1p = [mksb(es4, "x1p0", [128, 8 * 512], F32),
               mksb(es4, "x1p1", [128, 8 * 512], F32)]
        gh = mksb(es4, "gh", [128, 8], F32)
        dtl = [mksb(es4, "dtl0", [128, 512], F32),
               mksb(es4, "dtl1", [128, 512], F32)]
        xrb = [mksb(es4, "xrb0", [128, 8 * 512], BF16),
               mksb(es4, "xrb1", [128, 8 * 512], BF16)]
        xkb = [mksb(es4, "xkb0", [128, 8 * 512], BF16),
               mksb(es4, "xkb1", [128, 8 * 512], BF16)]
        rb = mksb(es4, "rb", [128, 8 * 512], BF16)
        kab = mksb(es4, "kab", [128, 8 * 512], BF16)
        a2 = mksb(es4, "a2", [128, 8 * 512], F32)
        ab2r = [mksb(es4, "ab2r0", [128, 512], BF16),
                mksb(es4, "ab2r1", [128, 512], BF16)]
        sqr = [mksb(es4, "sqr0", [128, 512], BF16),
               mksb(es4, "sqr1", [128, 512], BF16)]
        ms2 = mksb(es4, "ms2", [128, 512], F32)
        s2s2 = mksb(es4, "s2s2", [128, 512], F32)
        vs2 = mksb(es4, "vs2", [128, 512], F32)

        @blk.gpsimd
        def _(g):
            S.wait(g, "p3done")
            for wi, dw in enumerate([d_wrt, d_wkt, d_wvt]):
                for c in range(8):
                    S.inc(g.dma_start(
                        out=w_ffn[:, wi * 8192 + c * 1024:
                                  wi * 8192 + (c + 1) * 1024],
                        in_=dw[c * 128:(c + 1) * 128, :]), "swf", 16)

        mk4 = dict(x1ld={}, mixv={}, mixp={}, rkT={}, a2v={}, yst={},
                   l2appp={}, ln2v={})

        def emit_ffn_head(tile):
            par = tile % 2

            @blk.sync
            def _(sync):
                S.wait(sync, "x1st0")
                S.wait(sync, "x1st1")
                if tile >= 2:
                    S.wait(sync, "a2v", mk4["a2v"][tile - 2])
                for c in range(8):
                    S.inc(sync.dma_start(
                        out=x1g[par][:, c * 512:(c + 1) * 512],
                        in_=d_x1[c, :, tile * 512:(tile + 1) * 512]),
                        "x1ld", 16)

            mk4["x1ld"][tile] = S.val("x1ld")

            @blk.vector
            def _(v):
                S.wait(v, "x1ld", mk4["x1ld"][tile])
                if tile >= 2:
                    S.wait(v, "yst", mk4["yst"][tile - 2])
                    S.wait(v, "rkT", mk4["rkT"][tile - 2])
                if tile == 0:
                    S.inc(v.tensor_copy(gh[:, :], x1m1[:, :]), "mixv", 1)
                else:
                    S.inc(v.tensor_copy(gh[:, :],
                                        xlast[:, (tile - 1) * 8:tile * 8]),
                          "mixv", 1)
                S.wait(v, "mixv")
                xg = x1g[par][:, :].rearrange("p (c t) -> p c t", t=512)
                xp = x1p[par][:, :].rearrange("p (c t) -> p c t", t=512)
                S.inc(v.tensor_copy(xp[:, :, 1:512], xg[:, :, 0:511]),
                      "mixv", 1)
                S.inc(v.tensor_copy(
                    xp[:, :, 0:1],
                    gh[:, :].rearrange("p (c o) -> p c o", o=1)), "mixv", 1)
                S.wait(v, "mixv")
                for c in range(8):
                    gi = tile * 8 + c
                    S.inc(v.tensor_sub(dtl[gi % 2][:, :],
                                       x1g[par][:, c * 512:(c + 1) * 512],
                                       x1p[par][:, c * 512:(c + 1) * 512]),
                          "mixv", 1)
                    S.wait(v, "mixv")
                    S.inc(v.scalar_tensor_tensor(
                        xrb[par][:, c * 512:(c + 1) * 512],
                        dtl[gi % 2][:, :], c_mur[:, c:c + 1],
                        x1p[par][:, c * 512:(c + 1) * 512],
                        ALU.mult, ALU.add), "mixv", 1)
                    S.inc(v.scalar_tensor_tensor(
                        xkb[par][:, c * 512:(c + 1) * 512],
                        dtl[gi % 2][:, :], c_muk[:, c:c + 1],
                        x1p[par][:, c * 512:(c + 1) * 512],
                        ALU.mult, ALU.add), "mixv", 1)

            mk4["mixv"][tile] = S.val("mixv")

        def emit_ffn_body(tile):
            par = tile % 2
            for f in range(8):
                bA = p_F[:, (f % 2) * 512:(f % 2) * 512 + 512]
                bB = p_G[:, (f % 2) * 512:(f % 2) * 512 + 512]

                @blk.tensor
                def _(t, f=f, bA=bA, bB=bB):
                    if f == 0:
                        S.wait(t, "mixv", mk4["mixv"][tile])
                        if tile == 0:
                            S.wait(t, "swf")
                        else:
                            S.wait(t, "a2v", mk4["a2v"][tile - 1])
                            S.wait(t, "ln2v", mk4["ln2v"][tile - 1])
                    if f >= 2:
                        S.wait(t, "sigm", mk4[("sig", tile, f - 2)])
                        S.wait(t, "kactv", mk4[("kact", tile, f - 2)])
                    for c in range(8):
                        t.matmul(bA,
                                 w_ffn[:, c * 1024 + f * 128:
                                       c * 1024 + (f + 1) * 128],
                                 xrb[par][:, c * 512:(c + 1) * 512],
                                 start=(c == 0), stop=(c == 7))
                    for c in range(8):
                        inst = t.matmul(
                            bB,
                            w_ffn[:, 8192 + c * 1024 + f * 128:
                                  8192 + c * 1024 + (f + 1) * 128],
                            xkb[par][:, c * 512:(c + 1) * 512],
                            start=(c == 0), stop=(c == 7))
                    S.inc(inst, "rkT", 1)

                @blk.scalar
                def _(sc, f=f, bA=bA):
                    S.wait(sc, "rkT")
                    S.inc(sc.activation(rb[:, f * 512:(f + 1) * 512], bA,
                                        AF.Sigmoid, bias=0.0, scale=1.0),
                          "sigm", 1)
                    mk4[("sig", tile, f)] = S.val("sigm")

                @blk.vector
                def _(v, f=f, bB=bB):
                    S.wait(v, "rkT")
                    S.inc(v.tensor_scalar_max(kab[:, f * 512:(f + 1) * 512],
                                              bB, 0.0), "kactv", 1)
                    S.wait(v, "kactv")
                    S.inc(v.tensor_mul(kab[:, f * 512:(f + 1) * 512],
                                       kab[:, f * 512:(f + 1) * 512],
                                       kab[:, f * 512:(f + 1) * 512]),
                          "kactv", 1)
                    mk4[("kact", tile, f)] = S.val("kactv")

            mk4["rkT"][tile] = S.val("rkT")
            sig_end = S.val("sigm")
            kact_end = S.val("kactv")

            for f in range(8):
                bA = p_F[:, (f % 2) * 512:(f % 2) * 512 + 512]

                @blk.tensor
                def _(t, f=f, bA=bA):
                    if f == 0:
                        S.wait(t, "sigm", sig_end)
                        S.wait(t, "kactv", kact_end)
                    if f >= 2:
                        S.wait(t, "a2v", mk4[("a2mul", tile, f - 2)])
                    for c in range(8):
                        inst = t.matmul(
                            bA,
                            w_ffn[:, 16384 + c * 1024 + f * 128:
                                  16384 + c * 1024 + (f + 1) * 128],
                            kab[:, c * 512:(c + 1) * 512],
                            start=(c == 0), stop=(c == 7))
                    S.inc(inst, "fvT", 1)

                if mk4.get("pendst") is not None:
                    mk4["pendst"]()
                    mk4["pendst"] = None

                @blk.vector
                def _(v, f=f, bA=bA):
                    S.wait(v, "fvT")
                    if S.val("st2T") >= 1:
                        S.wait(v, "st2T", S.val("st2T") - 1)
                    S.inc(v.tensor_tensor(a2[:, f * 512:(f + 1) * 512], bA,
                                          rb[:, f * 512:(f + 1) * 512],
                                          ALU.mult), "a2v", 1)
                    mk4[("a2mul", tile, f)] = S.val("a2v")
                    S.wait(v, "a2v")
                    S.inc(v.tensor_add(a2[:, f * 512:(f + 1) * 512],
                                       a2[:, f * 512:(f + 1) * 512],
                                       x1g[par][:, f * 512:(f + 1) * 512]),
                          "a2v", 1)
                    mk4[("a2add", tile, f)] = S.val("a2v")
                    S.wait(v, "a2v")
                    S.inc(v.tensor_copy(ab2r[f % 2][:, :],
                                        a2[:, f * 512:(f + 1) * 512]),
                          "a2v", 1)

                @blk.scalar
                def _(sc, f=f):
                    S.wait(sc, "a2v", mk4[("a2add", tile, f)])
                    if S.val("st2T") >= 1:
                        S.wait(sc, "st2T", S.val("st2T") - 1)
                    S.inc(sc.activation(sqr[f % 2][:, :],
                                        a2[:, f * 512:(f + 1) * 512],
                                        AF.Square, bias=0.0, scale=1.0),
                          "sqp2", 1)

                def st_blk(f=f):
                    @blk.tensor
                    def _(t):
                        S.wait(t, "a2v")
                        S.wait(t, "sqp2")
                        t.matmul(p_G[:, 0:512], c_ones[:, 0:128],
                                 ab2r[f % 2][:, :], start=(f == 0),
                                 stop=(f == 7), skip_group_check=True)
                        inst = t.matmul(p_G[:, 512:1024], c_ones[:, 0:128],
                                        sqr[f % 2][:, :], start=(f == 0),
                                        stop=(f == 7), skip_group_check=True)
                        S.inc(inst, "st2T", 1)

                mk4["pendst"] = st_blk

            if mk4.get("pendst") is not None:
                mk4["pendst"]()
                mk4["pendst"] = None

            mk4["a2v"][tile] = S.val("a2v")

            @blk.vector
            def _(v):
                S.wait(v, "st2T")
                if tile >= 1:
                    S.wait(v, "l2appp", mk4["l2appp"][tile - 1])
                S.inc(v.tensor_scalar_mul(ms2[:, :], p_G[:, 0:512], 1.0 / E),
                      "ln2v", 1)
                S.inc(v.tensor_scalar_mul(s2s2[:, :], p_G[:, 512:1024],
                                          1.0 / E), "ln2v", 1)
                S.wait(v, "ln2v")
                S.inc(v.tensor_mul(vs2[:, :], ms2[:, :], ms2[:, :]),
                      "ln2v", 1)
                S.wait(v, "ln2v")
                S.inc(v.scalar_tensor_tensor(vs2[:, :], s2s2[:, :], 1e-5,
                                             vs2[:, :], ALU.add,
                                             ALU.subtract), "ln2v", 1)
                S.wait(v, "ln2v")

            @blk.scalar
            def _(sc):
                S.wait(sc, "ln2v")
                S.inc(sc.activation(s2s2[:, :], vs2[:, :], AF.Sqrt,
                                    bias=0.0, scale=1.0), "ln2a", 1)

            @blk.vector
            def _(v):
                S.wait(v, "ln2a")
                S.inc(v.reciprocal(vs2[:, :], s2s2[:, :]), "ln2v", 1)

            mk4["ln2v"][tile] = S.val("ln2v")

            for c in range(8):
                @blk.vector
                def _(v, c=c):
                    S.wait(v, "ln2v")
                    S.inc(v.tensor_sub(x1p[par][:, c * 512:(c + 1) * 512],
                                       a2[:, c * 512:(c + 1) * 512],
                                       ms2[:, :]), "l2app", 1)

                @blk.gpsimd
                def _(g_, c=c):
                    S.wait(g_, "l2app")
                    S.inc(g_.tensor_mul(x1p[par][:, c * 512:(c + 1) * 512],
                                        x1p[par][:, c * 512:(c + 1) * 512],
                                        vs2[:, :]), "l2appp", 1)

            mk4["l2appp"][tile] = S.val("l2appp")

            @blk.sync
            def _(sync):
                S.wait(sync, "l2appp", mk4["l2appp"][tile])
                for c in range(8):
                    S.inc(sync.dma_start(
                        out=d_y[c, :, tile * 512:(tile + 1) * 512],
                        in_=x1p[par][:, c * 512:(c + 1) * 512]), "yst", 16)

            mk4["yst"][tile] = S.val("yst")

        emit_ffn_head(0)
        for tile in range(NTILE):
            if tile + 1 < NTILE:
                emit_ffn_head(tile + 1)
            emit_ffn_body(tile)

        es4.close()

        @blk.sync
        def _(sync):
            S.wait(sync, "yst")
    except _PhaseCutExc:
        pass

    return nc



def _prep_inputs(inputs, cfg: Cfg):
    x = np.ascontiguousarray(np.asarray(inputs["L0_attn_x"], np.float32)[0])
    xT = np.ascontiguousarray(x.T)
    T, PCH = cfg.T, cfg.PCH
    qkv_w = np.ascontiguousarray(np.asarray(inputs["L0_attn_qkv_w"], np.float32))
    out_w = np.ascontiguousarray(np.asarray(inputs["L0_attn_out_w"], np.float32))
    wrT = np.ascontiguousarray(np.asarray(inputs["L0_ffn_W_r"], np.float32).T)
    wkT = np.ascontiguousarray(np.asarray(inputs["L0_ffn_W_k"], np.float32).T)
    wvT = np.ascontiguousarray(np.asarray(inputs["L0_ffn_W_v"], np.float32).T)
    qkv_b = np.asarray(inputs["L0_attn_qkv_b"], np.float32)
    out_b = np.asarray(inputs["L0_attn_out_b"], np.float32)
    mu_r = np.asarray(inputs["L0_ffn_mu_r"], np.float32)
    mu_k = np.asarray(inputs["L0_ffn_mu_k"], np.float32)

    brow = np.ascontiguousarray(qkv_b[None, :])
    outb_fm = np.ascontiguousarray(out_b.reshape(8, 128).T)
    mur_fm = np.ascontiguousarray(mu_r.reshape(8, 128).T)
    muk_fm = np.ascontiguousarray(mu_k.reshape(8, 128).T)

    ident = np.eye(128, dtype=np.float32).astype(BB)
    maskc = np.tile(np.triu(np.ones((CH, CH), np.float32)), (1, 4)).astype(BB)
    e31 = np.zeros((128, 31), np.float32)
    e31[:, 15] = 1.0
    e31 = e31.astype(BB)
    ebc = np.zeros((16, 8 * 128), np.float32)
    for c in range(8):
        for m in range(128):
            ebc[2 * c + m // 64, c * 128 + m] = 1.0
    ebc = ebc.astype(BB)
    ones = np.ones((128, 512), np.float32).astype(BB)

    in_maps = []
    for i in range(cfg.ncores):
        start = i * T
        xpfx = np.zeros((E, PCH * CH), np.float32)
        nreal = min(start, PCH * CH)
        if nreal > 0:
            xpfx[:, PCH * CH - nreal:] = xT[:, start - nreal:start]
        xsl = np.ascontiguousarray(xT[:, start:start + T])
        if start > 0:
            xm1 = np.ascontiguousarray(xT[:, start - 1].reshape(8, 128).T)
        else:
            xm1 = np.zeros((128, 8), np.float32)
        pmask = np.zeros((128, PCH), np.float32)
        nrealch = nreal // CH
        if nrealch > 0:
            pmask[:, PCH - nrealch:] = 1.0
        m0 = np.full((128, 1), 0.0 if i == 0 else 1.0, np.float32)
        in_maps.append(dict(
            xpfx=xpfx, xsl=xsl, xm1=xm1, pmask=pmask, m0=m0,
            qkv_w=qkv_w, out_w=out_w, wrT=wrT, wkT=wkT, wvT=wvT,
            brow=brow, outb_fm=outb_fm, mur_fm=mur_fm, muk_fm=muk_fm,
            ident=ident, maskc=maskc, e31=e31, ebc=ebc, ones=ones,
        ))
    return in_maps


def kernel(**inputs):
    cfg = Cfg()
    in_maps = _prep_inputs(inputs, cfg)
    nc = build_nc(cfg)
    res = run_bass_kernel_spmd(nc, in_maps, core_ids=list(range(cfg.ncores)))
    outs = []
    for i in range(cfg.ncores):
        y = np.asarray(res.results[i]["y"])
        outs.append(y.transpose(2, 0, 1).reshape(cfg.T, E))
    return np.concatenate(outs, axis=0)[None].astype(np.float32)


# version 24
# speedup vs baseline: 1.0081x; 1.0023x over previous
import numpy as np
import ml_dtypes
from contextlib import ExitStack

import concourse.bass as bass
import concourse.mybir as mybir
from concourse.bass_utils import run_bass_kernel_spmd

F32 = mybir.dt.float32
BF16 = mybir.dt.bfloat16
AF = mybir.ActivationFunctionType
ALU = mybir.AluOpType
AX = mybir.AxisListType
BB = ml_dtypes.bfloat16

E = 1024
H = 16
CH = 128


class Cfg:
    def __init__(self, T=2048, PCH=112, ncores=8):
        self.T = T
        self.PCH = PCH
        self.NTILE = T // 512
        self.NCHUNK = T // CH
        self.ncores = ncores


class Sems:
    def __init__(self, nc, es):
        self.nc, self.es = nc, es
        self.h, self.n = {}, {}

    def _get(self, name):
        if name not in self.h:
            self.h[name] = self.es.enter_context(self.nc.semaphore(name))
            self.n[name] = 0
        return self.h[name]

    def inc(self, inst, name, k=1):
        inst.then_inc(self._get(name), k)
        self.n[name] += k
        return self.n[name]

    def wait(self, eng, name, val=None):
        eng.wait_ge(self._get(name), self.n[name] if val is None else val)

    def val(self, name):
        self._get(name)
        return self.n[name]


class _PhaseCutExc(Exception):
    pass


def build_nc(cfg: Cfg, phases=4, p3sub=9, pskip=()):
    _PhaseCut = _PhaseCutExc
    nc = bass.Bass("TRN2", target_bir_lowering=False)
    T, PCH, NTILE, NCHUNK = cfg.T, cfg.PCH, cfg.NTILE, cfg.NCHUNK
    NPT = PCH // 4

    def din(name, shape, dtype):
        return nc.dram_tensor(name, shape, dtype, kind="ExternalInput").ap()

    d_xpfx = din("xpfx", [E, PCH * CH], F32)
    d_xsl = din("xsl", [E, T], F32)
    d_xm1 = din("xm1", [128, 8], F32)
    d_pmask = din("pmask", [128, PCH], F32)
    d_m0 = din("m0", [128, 1], F32)
    d_qkvw = din("qkv_w", [E, 3 * E], F32)
    d_outw = din("out_w", [E, E], F32)
    d_wrt = din("wrT", [E, E], F32)
    d_wkt = din("wkT", [E, E], F32)
    d_wvt = din("wvT", [E, E], F32)
    d_brow = din("brow", [1, 3 * E], F32)
    d_outb = din("outb_fm", [128, 8], F32)
    d_mur = din("mur_fm", [128, 8], F32)
    d_muk = din("muk_fm", [128, 8], F32)
    d_ident = din("ident", [128, 128], BF16)
    d_mask = din("maskc", [128, 4 * CH], BF16)
    d_e31 = din("e31", [128, 31], BF16)
    d_ebc = din("ebc", [16, 8 * 128], BF16)
    d_ones = din("ones", [128, 512], BF16)
    d_y = nc.dram_tensor("y", [8, 128, T], F32, kind="ExternalOutput").ap()
    d_x1 = nc.dram_tensor("x1d", [8, 128, T], F32).ap()

    es = ExitStack()
    try:
      with es:
        S = Sems(nc, es)
        blk = es.enter_context(nc.Block())

        def mksb(stack, name, shape, dtype):
            return stack.enter_context(nc.sbuf_tensor(name, shape, dtype))

        c_id = mksb(es, "c_id", [128, 128], BF16)
        c_mask = mksb(es, "c_mask", [128, 4 * CH], BF16)
        c_e31 = mksb(es, "c_e31", [128, 31], BF16)
        c_ebc = mksb(es, "c_ebc", [16, 8 * 128], BF16)
        c_ones = mksb(es, "c_ones", [128, 512], BF16)
        c_brow = mksb(es, "c_brow", [1, E], BF16)
        c_outb = mksb(es, "c_outb", [128, 8], F32)
        c_mur = mksb(es, "c_mur", [128, 8], F32)
        c_muk = mksb(es, "c_muk", [128, 8], F32)
        c_pm = mksb(es, "c_pm", [128, PCH], F32)
        c_m0 = mksb(es, "c_m0", [128, 1], F32)
        x1m1 = mksb(es, "x1m1", [128, 8], F32)
        xlast = mksb(es, "xlast", [128, 8 * NTILE], F32)

        es13 = ExitStack()
        pkm = mksb(es13, "pkm", [128, 128], BF16)
        xb = [mksb(es13, "xb0", [128, 8 * 512], BF16),
              mksb(es13, "xb1", [128, 8 * 512], BF16)]
        xstg = [mksb(es13, "xg0", [128, 512], F32),
                mksb(es13, "xg1", [128, 512], F32)]
        kexp = mksb(es13, "kexp", [128, 16 * 128], BF16)
        sexp = mksb(es13, "sexp", [128, 16 * 64], BF16)
        kfm = mksb(es13, "kfm", [128, 8 * T], BF16)
        vaug = mksb(es13, "vaug", [128, NCHUNK * H * 66], BF16)
        snaps = mksb(es13, "snaps", [128, NCHUNK * 528], BF16)
        sabs = mksb(es13, "sabs", [128, 8 * 65], F32)
        w_q = mksb(es13, "w_q", [128, 8 * 1024], BF16)
        w_out = mksb(es13, "w_out", [128, 8 * 1024], BF16)

        @blk.sync
        def _(sync):
            for src, dst in [(d_ident, c_id), (d_mask, c_mask), (d_e31, c_e31),
                             (d_ebc, c_ebc), (d_ones, c_ones), (d_outb, c_outb),
                             (d_mur, c_mur), (d_muk, c_muk), (d_pmask, c_pm),
                             (d_m0, c_m0)]:
                S.inc(sync.dma_start(out=dst[:, :], in_=src), "cl", 16)

        es1 = ExitStack()
        w_kv = mksb(es1, "w_kv", [128, 8 * 2048], BF16)
        c_brkv = mksb(es1, "c_brkv", [1, 2 * E], BF16)
        ktm = [mksb(es1, "ktm0", [128, E], BF16),
               mksb(es1, "ktm1", [128, E], BF16)]
        vtmp = [mksb(es1, "vtm0", [128, H * 66], BF16),
                mksb(es1, "vtm1", [128, H * 66], BF16)]
        _r1b = mksb(es1, "r1b0", [128, E], BF16)
        r1b = [_r1b, _r1b]
        _n1f = mksb(es1, "n1f0", [128, E], F32)
        n1f = [_n1f, _n1f]
        _ebb = mksb(es1, "eb0", [128, E], BF16)
        ebb = [_ebb, _ebb]

        @blk.gpsimd
        def _(g):
            S.inc(g.dma_start(out=c_brow[:, :], in_=d_brow[:, 0:E]), "swb", 16)
            S.inc(g.dma_start(out=c_brkv[:, :], in_=d_brow[:, E:]), "swb", 16)
            for c in range(8):
                S.inc(g.dma_start(out=w_kv[:, c * 2048:(c + 1) * 2048],
                                  in_=d_qkvw[c * 128:(c + 1) * 128, 1024:]),
                      "swkv", 16)
            for c in range(8):
                S.inc(g.dma_start(out=w_q[:, c * 1024:(c + 1) * 1024],
                                  in_=d_qkvw[c * 128:(c + 1) * 128, 0:1024]),
                      "swq", 16)
            for c in range(8):
                S.inc(g.dma_start(out=w_out[:, c * 1024:(c + 1) * 1024],
                                  in_=d_outw[c * 128:(c + 1) * 128, :]),
                      "swo", 16)

        @blk.vector
        def _(v):
            S.wait(v, "cl")
            va = vaug[:, :].rearrange("p (j s) -> p j s", s=66)
            S.inc(v.memset(va[:, :, 64:66], 0.0), "vc", 1)
            S.wait(v, "vc")
            S.inc(v.memset(va[:, :, 64:65], 1.0), "vc", 1)
            S.inc(v.memset(pkm[:, :], 0.0), "vc", 1)
            S.inc(v.memset(sabs[:, :], 0.0), "vc", 1)
            S.inc(v.memset(vtmp[0][:, :], 0.0), "vc", 1)
            S.inc(v.memset(vtmp[1][:, :], 0.0), "vc", 1)
            S.wait(v, "vc")
            for _vt in vtmp:
                vv = _vt[:, :].rearrange("p (h s) -> p h s", s=66)
                S.inc(v.memset(vv[:, :, 64:65], 1.0), "vc", 1)

        es1p = ExitStack()
        p_kv_k = es1p.enter_context(nc.psum_tensor("pkvk", [128, 1024], F32))
        p_kv_v = es1p.enter_context(nc.psum_tensor("pkvv", [128, 1024], F32))
        p_st = [es1p.enter_context(nc.psum_tensor(f"pst{i}", [128, 512], F32))
                for i in range(2)]
        p_tr = es1p.enter_context(nc.psum_tensor("ptr", [128, 1024], BF16))

        mk = dict(eluv=[], elua=[], stT=[], xc={}, pkfree=[], cmin=0)

        def emit_xload(tile, which, key):
            par = tile % 2
            src = d_xpfx if which == "pfx" else d_xsl
            for c in range(8):
                @blk.sync
                def _(sync, c=c):
                    S.wait(sync, "xc", max(0, S.val("xc") - 1))
                    S.inc(sync.dma_start(
                        out=xstg[c % 2][:, :],
                        in_=src[c * 128:(c + 1) * 128,
                                tile * 512:(tile + 1) * 512]),
                        "xl%d" % (c % 2), 16)

                @blk.gpsimd
                def _(g_, c=c):
                    S.wait(g_, "xl%d" % (c % 2))
                    S.wait(g_, "projT", S.val("projT"))
                    S.wait(g_, "qprojT", S.val("qprojT"))
                    S.inc(g_.tensor_copy(xb[par][:, c * 512:(c + 1) * 512],
                                         xstg[c % 2][:, :]), "xc", 1)
            mk["xc"][key] = S.val("xc")

        def emit_kv_chunk(j, xtile_par, jj, masked_col, own_j, xc_key,
                          vbias=True):
            par = j % 2

            def xchunk(c):
                return xb[xtile_par][:, c * 512 + jj * 128:
                                     c * 512 + jj * 128 + 128]

            @blk.tensor
            def _(t):
                if jj == 0:
                    S.wait(t, "xc", mk["xc"][xc_key])
                if mk["pkfree"]:
                    S.wait(t, "eluv", mk["pkfree"][-1])
                if "proj" in pskip:
                    S.inc(t.nop(), "projK", 1)
                    S.inc(t.nop(), "projT", 2)
                    return
                for half in range(2):
                    for c in range(8):
                        t.matmul(p_kv_k[:, half * 512:(half + 1) * 512],
                                 xchunk(c),
                                 w_kv[:, c * 2048 + half * 512:
                                      c * 2048 + (half + 1) * 512],
                                 start=(c == 0), stop=False)
                    inst = t.matmul(p_kv_k[:, half * 512:(half + 1) * 512],
                             c_ones[0:1, 0:128],
                             c_brkv[:, half * 512:(half + 1) * 512],
                             start=False, stop=True)
                S.inc(inst, "projK", 1)
                if mk["eluv"]:
                    S.wait(t, "eluv", mk["eluv"][-1])
                for half in range(2):
                    for c in range(8):
                        inst = t.matmul(
                            p_kv_v[:, half * 512:(half + 1) * 512],
                            xchunk(c),
                            w_kv[:, c * 2048 + 1024 + half * 512:
                                 c * 2048 + 1024 + (half + 1) * 512],
                            start=(c == 0),
                            stop=(c == 7 and not vbias))
                    if vbias:
                        inst = t.matmul(p_kv_v[:, half * 512:(half + 1) * 512],
                                        c_ones[0:1, 0:128],
                                        c_brkv[:, 1024 + half * 512:
                                               1024 + (half + 1) * 512],
                                        start=False, stop=True)
                    S.inc(inst, "projT", 1)

            @blk.vector
            def _(v):
                S.wait(v, "projK")
                S.wait(v, "elua")
                if "elu" in pskip:
                    S.inc(v.nop(), "eluv", 2)
                    mk["cmin"] = S.val("eluv")
                    return
                S.inc(v.tensor_scalar_min(n1f[par][:, :], p_kv_k[:, :], 0.0),
                      "eluv", 1)
                mk["cmin"] = S.val("eluv")
                S.inc(v.tensor_scalar_max(r1b[par][:, :], p_kv_k[:, :], 0.0),
                      "eluv", 1)

            mk["pkfree"].append(S.val("eluv"))

            @blk.scalar
            def _(sc):
                S.wait(sc, "eluv", mk["cmin"])
                if "act" in pskip:
                    S.inc(sc.nop(), "elua", 1)
                    return
                i1 = sc.activation(ebb[par][:, :], n1f[par][:, :], AF.Exp,
                                   bias=0.0, scale=1.0)
                S.inc(i1, "elua", 1)

            @blk.vector
            def _(v):
                S.wait(v, "elua")
                S.wait(v, "eluv", mk["pkfree"][-1])
                if mk["stT"]:
                    S.wait(v, "stT", mk["stT"][-1])
                if "elu" in pskip:
                    for _i in range(4 if masked_col is not None else 3):
                        S.inc(v.nop(), "eluv", 1)
                    return
                S.inc(v.tensor_add(ktm[par][:, :], r1b[par][:, :],
                                   ebb[par][:, :]), "eluv", 1)
                if masked_col is not None:
                    S.wait(v, "eluv")
                    S.inc(v.tensor_scalar_mul(
                        ktm[par][:, :], ktm[par][:, :],
                        c_pm[:, masked_col:masked_col + 1]), "eluv", 1)
                if own_j is None:
                    vdst = vtmp[par][:, :].rearrange("p (h s) -> p h s", s=66)
                else:
                    vdst = vaug[:, own_j * H * 66:(own_j + 1) * H * 66
                                ].rearrange("p (h s) -> p h s", s=66)
                for half in range(2):
                    S.wait(v, "projT", S.val("projT") - (1 - half))
                    S.inc(v.tensor_copy(
                        vdst[:, half * 8:(half + 1) * 8, 0:64],
                        p_kv_v[:, half * 512:(half + 1) * 512].rearrange(
                            "p (h s) -> p h s", s=64)), "eluv", 1)

            mk["eluv"].append(S.val("eluv"))
            mk["elua"].append(S.val("elua"))
            eluv_end = S.val("eluv")

            def do_state():
                if own_j is not None:
                    @blk.tensor
                    def _(t):
                        S.wait(t, "eluv", eluv_end)
                        S.wait(t, "ktr", S.val("ktr"))
                        for c in range(8):
                            inst = t.transpose(
                                p_tr[:, c * 128:(c + 1) * 128],
                                ktm[par][:, c * 128:(c + 1) * 128],
                                c_id[:, :])
                        S.inc(inst, "trT", 1)

                    @blk.vector
                    def _(v):
                        S.wait(v, "trT")
                        kdst = kfm[:, :].rearrange("p (c t) -> p c t", t=T)
                        S.inc(v.tensor_copy(
                            kdst[:, :, own_j * 128:(own_j + 1) * 128],
                            p_tr[:, :].rearrange("p (c t) -> p c t", t=128)),
                            "ktr", 1)

                @blk.tensor
                def _(t):
                    S.wait(t, "eluv", eluv_end)
                    if "state" in pskip:
                        S.inc(t.nop(), "stT", 1)
                        return
                    if own_j is not None:
                        S.wait(t, "snapv", S.val("snapv"))
                    vsrc = (vtmp[par] if own_j is None else
                            vaug[:, own_j * H * 66:(own_j + 1) * H * 66])
                    for h in range(H):
                        bank = p_st[h // 8]
                        pair_col = ((h % 8) // 2) * 65
                        half = h % 2
                        if own_j is None:
                            st = (j == 0 and h in (0, 1, 8, 9))
                            sp = False
                        else:
                            st = h in (0, 1, 8, 9)
                            sp = h in (6, 7, 14, 15)
                        inst = t.matmul(
                            bank[half * 64:half * 64 + 64,
                                 pair_col:pair_col + 65],
                            ktm[par][:, h * 64:(h + 1) * 64],
                            vsrc[:, h * 66:h * 66 + 65],
                            start=st, stop=sp, skip_group_check=True)
                    S.inc(inst, "stT", 1)

                mk["stT"].append(S.val("stT"))

                if own_j is not None:
                    @blk.vector
                    def _(v):
                        S.wait(v, "stT")
                        sdst = snaps[:, own_j * 528:(own_j + 1) * 528]
                        S.inc(v.tensor_copy(
                            sdst.rearrange("p (c s) -> p c s",
                                           s=66)[:, :, 0:65],
                            sabs[:, :].rearrange("p (c s) -> p c s", s=65)),
                            "snapv", 1)
                        S.wait(v, "snapv")
                        for b in range(2):
                            inst = v.tensor_add(
                                sabs[:, b * 260:(b + 1) * 260],
                                sabs[:, b * 260:(b + 1) * 260],
                                p_st[b][:, 0:260])
                            S.inc(inst, "snapv", 1)
                        S.wait(v, "snapv")

            return do_state

        @blk.tensor
        def _(t):
            S.wait(t, "swkv")
            S.wait(t, "swb")
            S.wait(t, "cl")

        @blk.vector
        def _(v):
            S.wait(v, "cl")
            S.wait(v, "vc")

        emit_xload(0, "pfx", ("pfx", 0))
        pend = None
        for j in range(PCH):
            if j % 4 == 0 and j // 4 + 1 < NPT:
                emit_xload(j // 4 + 1, "pfx", ("pfx", j // 4 + 1))
            nxt = emit_kv_chunk(j, (j // 4) % 2, j % 4, masked_col=j,
                                own_j=None, xc_key=("pfx", j // 4),
                                vbias=False)
            if pend is not None:
                pend()
            pend = nxt
        pend()

        rowT2 = mksb(es1, "rowT2", [1, H * 64], BF16)
        scc = mksb(es1, "scc", [128, 8], F32)

        @blk.vector
        def _(v):
            S.wait(v, "stT")
            for b in range(2):
                S.inc(v.tensor_copy(scc[:, b * 4:(b + 1) * 4],
                                    p_st[b][:, 64:260:65]), "vbc", 1)

        @blk.gpsimd
        def _(g):
            S.wait(g, "vbc")
            for h in range(H):
                b, half = h // 8, h % 2
                pr = (h % 8) // 2
                S.inc(g.dma_start(
                    out=rowT2[0:1, h * 64:(h + 1) * 64],
                    in_=scc[half * 64:half * 64 + 64, b * 4 + pr:
                            b * 4 + pr + 1]),
                    "vbf", 16)

        @blk.tensor
        def _(t):
            S.wait(t, "vbf")
            for h in range(H):
                b, half = h // 8, h % 2
                pair_col = ((h % 8) // 2) * 65
                inst = t.matmul(
                    p_st[b][half * 64:half * 64 + 64,
                            pair_col:pair_col + 64],
                    rowT2[0:1, h * 64:(h + 1) * 64],
                    c_brkv[0:1, 1024 + h * 64:1024 + (h + 1) * 64],
                    start=False, stop=(h in (6, 7, 14, 15)),
                    skip_group_check=True)
            S.inc(inst, "stT", 1)

        @blk.vector
        def _(v):
            S.wait(v, "stT")
            for b in range(2):
                S.inc(v.tensor_copy(sabs[:, b * 260:(b + 1) * 260],
                                    p_st[b][:, 0:260]), "snapv", 1)
            S.wait(v, "snapv")

        pend = None
        for j in range(NCHUNK):
            if j % 4 == 0:
                emit_xload(j // 4, "own", ("own", j // 4))
            nxt = emit_kv_chunk(j, (j // 4) % 2, j % 4, masked_col=None,
                                own_j=j, xc_key=("own", j // 4))
            if pend is not None:
                pend()
            pend = nxt
        pend()

        @blk.tensor
        def _(t):
            S.wait(t, "eluv", S.val("eluv"))
            S.wait(t, "elua", S.val("elua"))
            S.wait(t, "snapv", S.val("snapv"))
            S.wait(t, "ktr", S.val("ktr"))
            S.inc(t.nop(), "p1done", 1)

        es1p.close()
        es1.close()

        if phases < 2:
            raise _PhaseCut
        es2 = ExitStack()
        p_a8 = es2.enter_context(nc.psum_tensor("pm1a", [128, 512], F32))
        p_b8 = es2.enter_context(nc.psum_tensor("pm1b", [128, 512], F32))
        p_n8 = es2.enter_context(nc.psum_tensor("pm1n", [16, 512], F32))

        m1_x = mksb(es2, "m1x", [128, 8], F32)
        m1_xb = mksb(es2, "m1xb", [128, 8], BF16)
        m1_q = mksb(es2, "m1q", [128, 8], BF16)
        m1_r1 = mksb(es2, "m1r1", [128, 8], BF16)
        m1_n1 = mksb(es2, "m1n1", [128, 8], F32)
        m1_e = mksb(es2, "m1e", [128, 8], BF16)
        m1_rn = mksb(es2, "m1rn", [16, 2], F32)
        m1_rnb = mksb(es2, "m1rnb", [16, 1], BF16)
        m1_rbc = mksb(es2, "m1rbc", [128, 8], BF16)
        m1_ao = mksb(es2, "m1ao", [128, 8], BF16)
        m1_a = mksb(es2, "m1a", [128, 8], F32)
        m1_ab = mksb(es2, "m1ab", [128, 8], BF16)
        m1_sq = mksb(es2, "m1sq", [128, 8], BF16)
        m1_s1 = mksb(es2, "m1s1", [128, 1], F32)
        m1_s2 = mksb(es2, "m1s2", [128, 1], F32)
        m1_t0 = mksb(es2, "m1t0", [128, 1], F32)
        m1_t1 = mksb(es2, "m1t1", [128, 1], F32)
        m1_se = mksb(es2, "m1se", [128, 16 * 64], BF16)

        @blk.sync
        def _(sync):
            S.wait(sync, "p1done")
            S.inc(sync.dma_start(out=m1_x[:, :], in_=d_xm1), "m1l", 16)

        @blk.vector
        def _(v):
            S.wait(v, "m1l")
            S.inc(v.tensor_copy(m1_xb[:, :], m1_x[:, :]), "m1v", 1)

        @blk.tensor
        def _(t):
            S.wait(t, "p1done")
            S.wait(t, "swq")
            S.wait(t, "m1v")
            for f in range(8):
                for c in range(8):
                    t.matmul(p_a8[:, f:f + 1],
                             w_q[:, c * 1024 + f * 128:
                                 c * 1024 + (f + 1) * 128],
                             m1_xb[:, c:c + 1],
                             start=(c == 0), stop=False)
                inst = t.matmul(p_a8[:, f:f + 1],
                                c_brow[0:1, f * 128:(f + 1) * 128],
                                c_ones[0:1, 0:1],
                                start=False, stop=True)
            S.inc(inst, "m1T", 1)

        @blk.vector
        def _(v):
            S.wait(v, "m1T")
            S.inc(v.tensor_scalar_min(m1_n1[:, :], p_a8[:, 0:8], 0.0),
                  "m1v", 1)
            mk["m1cmin"] = S.val("m1v")
            S.inc(v.tensor_scalar_max(m1_r1[:, :], p_a8[:, 0:8], 0.0), "m1v", 1)
            mk["m1max"] = S.val("m1v")

        @blk.scalar
        def _(sc):
            S.wait(sc, "m1v", mk["m1cmin"])
            i1 = sc.activation(m1_e[:, :], m1_n1[:, :], AF.Exp,
                               bias=0.0, scale=1.0)
            S.inc(i1, "m1a", 1)

        @blk.vector
        def _(v):
            S.wait(v, "m1a")
            S.wait(v, "m1v", mk["m1max"])
            S.inc(v.tensor_add(m1_q[:, :], m1_r1[:, :], m1_e[:, :]), "m1v", 1)
            S.inc(v.memset(m1_se[:, :], 0.0), "m1v", 1)
            S.wait(v, "m1v")
            se = m1_se[:, :].rearrange("p (h d) -> p h d", d=64)
            sn = snaps[:, 0:528].rearrange("p (c s) -> p c s", s=66)
            for half in range(2):
                sl = slice(half * 64, half * 64 + 64)
                S.inc(v.tensor_copy(se[sl, half::2, :], sn[sl, :, 0:64]),
                      "m1v", 1)
            for half in range(2):
                src = snaps[half * 64:(half + 1) * 64,
                            64:64 + 7 * 66 + 1:66]
                dst = pkm[half * 64:(half + 1) * 64,
                          half:half + 7 * 18 + 1:18]
                inst = v.tensor_copy(dst, src)
            S.inc(inst, "m1v", 1)

        @blk.tensor
        def _(t):
            S.wait(t, "m1v")
            S.wait(t, "m1a")
            for h in range(H):
                c, half = h // 2, h % 2
                t.matmul(p_b8[half * 64:half * 64 + 64, c:c + 1],
                         m1_se[:, h * 64:(h + 1) * 64],
                         m1_q[:, c:c + 1],
                         start=(h in (0, 1)), stop=(h in (14, 15)),
                         skip_group_check=True)
            for c in range(8):
                inst = t.matmul(p_n8[:, c:c + 1], pkm[:, 16 * c:16 * c + 16],
                                m1_q[:, c:c + 1],
                                start=(c == 0), stop=(c == 7))
            S.inc(inst, "m1T", 1)

        @blk.vector
        def _(v):
            S.wait(v, "m1T")
            S.inc(v.reduce_sum(m1_rn[:, 0:1], p_n8[:, 0:8], axis=AX.X), "m1v", 1)
            S.wait(v, "m1v")
            S.inc(v.tensor_scalar_add(m1_rn[:, 0:1], m1_rn[:, 0:1], 1e-6),
                  "m1v", 1)
            S.wait(v, "m1v")
            S.inc(v.reciprocal(m1_rn[:, 1:2], m1_rn[:, 0:1]), "m1v", 1)
            S.wait(v, "m1v")
            S.inc(v.tensor_copy(m1_rnb[:, :], m1_rn[:, 1:2]), "m1v", 1)

        @blk.tensor
        def _(t):
            S.wait(t, "m1v")
            for c in range(8):
                inst = t.matmul(p_a8[:, c:c + 1],
                                c_ebc[:, c * 128:(c + 1) * 128],
                                m1_rnb[:, :],
                                start=(c == 0), stop=(c == 7))
            S.inc(inst, "m1T", 1)

        @blk.vector
        def _(v):
            S.wait(v, "m1T")
            S.inc(v.tensor_copy(m1_rbc[:, :], p_a8[:, 0:8]), "m1v", 1)
            S.wait(v, "m1v")
            S.inc(v.tensor_tensor(m1_ao[:, :], p_b8[:, 0:8], m1_rbc[:, :],
                                  ALU.mult), "m1v", 1)

        @blk.tensor
        def _(t):
            S.wait(t, "m1v")
            S.wait(t, "swo")
            for f in range(8):
                for c in range(8):
                    inst = t.matmul(p_b8[:, f:f + 1],
                                    w_out[:, c * 1024 + f * 128:
                                          c * 1024 + (f + 1) * 128],
                                    m1_ao[:, c:c + 1],
                                    start=(c == 0), stop=(c == 7))
            S.inc(inst, "m1T", 1)

        @blk.vector
        def _(v):
            S.wait(v, "m1T")
            S.inc(v.scalar_tensor_tensor(m1_a[:, :], p_b8[:, 0:8], 1.0,
                                         m1_x[:, :], ALU.mult, ALU.add),
                  "m1v", 1)
            S.wait(v, "m1v")
            for c in range(8):
                inst = v.tensor_scalar_add(m1_a[:, c:c + 1], m1_a[:, c:c + 1],
                                           c_outb[:, c:c + 1])
            S.inc(inst, "m1v", 1)
            S.wait(v, "m1v")
            S.inc(v.tensor_copy(m1_ab[:, :], m1_a[:, :]), "m1v", 1)
            S.wait(v, "m1v")
            S.inc(v.tensor_mul(m1_sq[:, :], m1_ab[:, :], m1_ab[:, :]),
                  "m1v", 1)

        @blk.tensor
        def _(t):
            S.wait(t, "m1v")
            for c in range(8):
                t.matmul(p_a8[:, 0:1], c_ones[:, 0:128], m1_ab[:, c:c + 1],
                         start=(c == 0), stop=(c == 7))
            for c in range(8):
                inst = t.matmul(p_a8[:, 1:2], c_ones[:, 0:128],
                                m1_sq[:, c:c + 1],
                                start=(c == 0), stop=(c == 7))
            S.inc(inst, "m1T", 1)

        @blk.vector
        def _(v):
            S.wait(v, "m1T")
            S.inc(v.tensor_scalar_mul(m1_s1[:, :], p_a8[:, 0:1], 1.0 / E),
                  "m1v", 1)
            S.inc(v.tensor_scalar_mul(m1_s2[:, :], p_a8[:, 1:2], 1.0 / E),
                  "m1v", 1)
            S.wait(v, "m1v")
            S.inc(v.tensor_mul(m1_t0[:, :], m1_s1[:, :], m1_s1[:, :]),
                  "m1v", 1)
            S.wait(v, "m1v")
            S.inc(v.scalar_tensor_tensor(m1_t1[:, :], m1_s2[:, :], 1e-5,
                                         m1_t0[:, :], ALU.add,
                                         ALU.subtract), "m1v", 1)
            S.wait(v, "m1v")

        @blk.scalar
        def _(sc):
            S.wait(sc, "m1v")
            S.inc(sc.activation(m1_t1[:, :], m1_t1[:, :], AF.Sqrt,
                                bias=0.0, scale=1.0), "m1a", 1)

        @blk.vector
        def _(v):
            S.wait(v, "m1a")
            S.inc(v.reciprocal(m1_t0[:, :], m1_t1[:, :]), "m1v", 1)
            S.wait(v, "m1v")
            S.inc(v.tensor_scalar(x1m1[:, :], m1_a[:, :], m1_s1[:, 0:1],
                                  None, ALU.subtract), "m1v", 1)
            S.wait(v, "m1v")
            S.inc(v.tensor_scalar_mul(x1m1[:, :], x1m1[:, :], m1_t0[:, 0:1]),
                  "m1v", 1)
            S.wait(v, "m1v")
            S.inc(v.tensor_scalar_mul(x1m1[:, :], x1m1[:, :], c_m0[:, 0:1]),
                  "m1v", 1)
            S.wait(v, "m1v")

        @blk.tensor
        def _(t):
            S.wait(t, "m1v")
            S.wait(t, "m1a")
            S.inc(t.nop(), "p2done", 1)

        es2.close()

        if phases < 3:
            raise _PhaseCut
        es3 = ExitStack()
        p_A = es3.enter_context(nc.psum_tensor("pA", [128, 1024], F32))
        p_C = es3.enter_context(nc.psum_tensor("pC", [128, 1024], F32))
        p_N = es3.enter_context(nc.psum_tensor("pN", [16, 512], F32))
        p_R = es3.enter_context(nc.psum_tensor("pR", [128, 1024], F32))
        qp = mksb(es3, "qp", [128, 8 * 512], BF16)
        qn1 = [mksb(es3, "qn1a", [128, 512], F32),
               mksb(es3, "qn1b", [128, 512], F32)]
        qr1 = [mksb(es3, "qr1a", [128, 512], BF16),
               mksb(es3, "qr1b", [128, 512], BF16)]
        qe = [mksb(es3, "qea", [128, 512], BF16),
              mksb(es3, "qeb", [128, 512], BF16)]
        sbT = mksb(es3, "sbTt", [128, 2048], BF16)
        rn16 = mksb(es3, "rn16", [16, 128], F32)
        rn16b = mksb(es3, "rn16b", [16, 128], BF16)
        rnbc = mksb(es3, "rnbc", [128, 1024], BF16)
        attnb = mksb(es3, "attnb", [128, 8 * 512], BF16)
        ab = [mksb(es3, "ab0", [128, 512], BF16),
              mksb(es3, "ab1", [128, 512], BF16)]
        sqb = [mksb(es3, "sqb0", [128, 512], BF16),
               mksb(es3, "sqb1", [128, 512], BF16)]
        msb = mksb(es3, "msb", [128, 512], F32)
        vsb = mksb(es3, "vsb", [128, 512], F32)
        s2sb = qn1[0]
        af = mksb(es3, "af", [128, 8 * 512], F32)
        xres = [mksb(es3, "xr0", [128, 512], F32),
                mksb(es3, "xr1", [128, 512], F32)]
        x1c = [qn1[1], qn1[1]]

        @blk.vector
        def _(v):
            S.wait(v, "m1v")
            S.wait(v, "m1a")
            S.inc(v.memset(kexp[:, :], 0.0), "kexpz", 1)
            S.inc(v.memset(sexp[:, :], 0.0), "kexpz", 1)

        for tile in range(NTILE):
            emit_xload(tile, "own", ("main", tile))

            for fp in range(4):
                for fo in range(2):
                    f = 2 * fp + fo
                    bs = (p_A, p_A, p_C, p_C)[f % 4][
                        :, (f % 2) * 512:(f % 2) * 512 + 512]
                    mk[("qbs", f % 2)] = bs

                    @blk.tensor
                    def _(t, f=f, bs=bs, tile=tile):
                        S.wait(t, "xc", mk["xc"][("main", tile)])
                        if tile == 0 and f == 0:
                            S.wait(t, "p2done")
                        if f < 4:
                            S.wait(t, "resv", S.val("resv"))
                            S.wait(t, "lnv", S.val("lnv"))
                            S.wait(t, "lapp", S.val("lapp"))
                            S.wait(t, "x1st0", S.val("x1st0"))
                            S.wait(t, "x1st1", S.val("x1st1"))
                        else:
                            S.wait(t, "qelu", mk[("qmax", tile, f - 4)])
                        par = tile % 2
                        for c in range(8):
                            t.matmul(bs,
                                     w_q[:, c * 1024 + f * 128:
                                         c * 1024 + (f + 1) * 128],
                                     xb[par][:, c * 512:(c + 1) * 512],
                                     start=(c == 0), stop=False)
                        inst = t.matmul(bs,
                                        c_brow[0:1, f * 128:(f + 1) * 128],
                                        c_ones[0:1, 0:512],
                                        start=False, stop=True)
                        S.inc(inst, "qprojT", 1)

                    @blk.vector
                    def _(v, f=f, fo=fo, bs=bs, tile=tile):
                        S.wait(v, "qprojT")
                        S.wait(v, "qelua")
                        S.inc(v.tensor_scalar_min(qn1[fo][:, :], bs, 0.0),
                              "qelu", 1)
                        mk[("qcmin", tile, f)] = S.val("qelu")
                        S.inc(v.tensor_scalar_max(qr1[fo][:, :], bs, 0.0),
                              "qelu", 1)
                        mk[("qmax", tile, f)] = S.val("qelu")

                @blk.scalar
                def _(sc, fp=fp, tile=tile):
                    for fo in range(2):
                        S.wait(sc, "qelu", mk[("qcmin", tile, 2 * fp + fo)])
                        S.inc(sc.activation(qe[fo][:, :], qn1[fo][:, :],
                                            AF.Exp, bias=0.0, scale=1.0),
                              "qelua", 1)

                for fo in range(2):
                    @blk.vector
                    def _(v, f=2 * fp + fo, fo=fo):
                        S.wait(v, "qelua", S.val("qelua") - (1 - fo))
                        S.wait(v, "qelu", mk[("qmax", tile, 2 * fp + fo)])
                        S.inc(v.tensor_add(qp[:, f * 512:(f + 1) * 512],
                                           qr1[fo][:, :], qe[fo][:, :]),
                              "qelu", 1)

            for jj in range(4 if p3sub >= 2 else 0):
                jglob = tile * 4 + jj

                @blk.vector
                def _(v, jglob=jglob):
                    S.wait(v, "qelu")
                    S.wait(v, "normT", S.val("normT"))
                    if jglob == 0:
                        S.wait(v, "kexpz")
                    S.wait(v, "scoreT", S.val("scoreT"))
                    S.wait(v, "svT", S.val("svT"))
                    ke = kexp[:, :].rearrange("p (h t) -> p h t", t=128)
                    kf = kfm[:, :].rearrange("p (c t) -> p c t", t=T)
                    se = sexp[:, :].rearrange("p (h d) -> p h d", d=64)
                    sn = snaps[:, jglob * 528:(jglob + 1) * 528].rearrange(
                        "p (c s) -> p c s", s=66)
                    for half in range(2):
                        sl = slice(half * 64, half * 64 + 64)
                        S.inc(v.tensor_copy(
                            ke[sl, half::2, :],
                            kf[sl, :, jglob * 128:(jglob + 1) * 128]),
                            "pkmv", 1)
                        S.inc(v.tensor_copy(se[sl, half::2, :],
                                            sn[sl, :, 0:64]), "pkmv", 1)
                    if p3sub != 22:
                        for half in range(2):
                            src = snaps[half * 64:(half + 1) * 64,
                                        jglob * 528 + 64:
                                        jglob * 528 + 64 + 7 * 66 + 1:66]
                            dst = pkm[half * 64:(half + 1) * 64,
                                      half:half + 7 * 18 + 1:18]
                            inst = v.tensor_copy(dst, src)
                    else:
                        inst = v.memset(pkm[:, 0:1], 0.0)
                    S.inc(inst, "pkmv", 1)

                for wave in range(4 if p3sub != 21 else 0):
                    bnk = (wave % 2) * 512

                    @blk.tensor
                    def _(t, jglob=jglob, jj=jj, wave=wave, bnk=bnk):
                        if wave == 0:
                            S.wait(t, "qelu")
                            S.wait(t, "pkmv")
                        S.wait(t, "maskv", max(0, S.val("maskv") - 1))
                        for i in range(4):
                            h = wave * 4 + i
                            c = h // 2
                            inst = t.matmul(
                                p_A[:, bnk + i * 128:bnk + (i + 1) * 128],
                                kexp[:, h * 128:(h + 1) * 128],
                                qp[:, c * 512 + jj * 128:
                                   c * 512 + jj * 128 + 128],
                                start=(i == 0), stop=(i == 3))
                        S.inc(inst, "scoreT", 1)

                    @blk.vector
                    def _(v, wave=wave, bnk=bnk):
                        S.wait(v, "scoreT")
                        S.wait(v, "svT", S.val("svT"))
                        S.wait(v, "normT", S.val("normT"))
                        if p3sub == 23:
                            inst = v.memset(sbT[:, 0:1], 0.0)
                        else:
                            inst = v.tensor_tensor(
                                sbT[:, wave * 512:(wave + 1) * 512],
                                p_A[:, bnk:bnk + 512], c_mask[:, 0:512],
                                ALU.mult)
                        S.inc(inst, "maskv", 1)

                if p3sub < 3:
                    continue

                if mk.get("pendrn") is not None:
                    mk["pendrn"]()
                    mk["pendrn"] = None
                if mk.get("pend3") is not None:
                    mk["pend3"]()
                    mk["pend3"] = None

                @blk.tensor
                def _(t, jglob=jglob, jj=jj):
                    S.wait(t, "pkmv")
                    S.wait(t, "aov", S.val("aov"))
                    S.wait(t, "lnv", S.val("lnv"))
                    for h in range(H):
                        c, half = h // 2, h % 2
                        t.matmul(
                            p_C[half * 64:half * 64 + 64,
                                c * 128:(c + 1) * 128],
                            sexp[:, h * 64:(h + 1) * 64],
                            qp[:, c * 512 + jj * 128:c * 512 + jj * 128 + 128],
                            start=(h in (0, 1, 8, 9)), stop=False,
                            skip_group_check=True)
                    S.wait(t, "rn16v", S.val("rn16v"))
                    for c in range(8):
                        t.matmul(p_N[:, 0:128],
                                 pkm[:, 16 * c:16 * c + 16],
                                 qp[:, c * 512 + jj * 128:
                                    c * 512 + jj * 128 + 128],
                                 start=(c == 0), stop=False)
                    S.wait(t, "maskv")
                    for h in list(range(0, H, 2)) + list(range(1, H, 2)):
                        c, half = h // 2, h % 2
                        inst = t.matmul(p_C[half * 64:half * 64 + 64,
                                            c * 128:(c + 1) * 128],
                                        vaug[:, jglob * H * 66 + h * 66:
                                             jglob * H * 66 + h * 66 + 64],
                                        sbT[:, h * 128:(h + 1) * 128],
                                        start=False,
                                        stop=(h in (6, 7, 14, 15)),
                                        skip_group_check=True)
                    S.inc(inst, "svT", 1)
                    for h in range(H):
                        inst = t.matmul(p_N[:, 0:128],
                                        c_e31[:, 15 - h:31 - h],
                                        sbT[:, h * 128:(h + 1) * 128],
                                        start=False, stop=(h == 15))
                    S.inc(inst, "normT", 1)

                def rnchain3():
                    @blk.vector
                    def _(v):
                        S.wait(v, "normT")
                        S.wait(v, "rnT", S.val("rnT"))
                        S.inc(v.tensor_scalar_add(rn16[:, :], p_N[:, 0:128],
                                                  1e-6), "rn16v", 1)
                        S.wait(v, "rn16v")
                        S.inc(v.reciprocal(rn16[:, :], rn16[:, :]),
                              "rn16v", 1)
                        S.wait(v, "rn16v")
                        S.inc(v.tensor_copy(rn16b[:, :], rn16[:, :]),
                              "rn16v", 1)

                def tail3(jj=jj):
                    @blk.tensor
                    def _(t):
                        S.wait(t, "rn16v")
                        S.wait(t, "aov", S.val("aov"))
                        for c in range(8):
                            inst = t.matmul(p_R[:, c * 128:(c + 1) * 128],
                                            c_ebc[:, c * 128:(c + 1) * 128],
                                            rn16b[:, :],
                                            start=(c in (0, 4)),
                                            stop=(c in (3, 7)))
                        S.inc(inst, "rnT", 1)

                    @blk.vector
                    def _(v):
                        S.wait(v, "rnT")
                        S.wait(v, "svT")
                        S.wait(v, "oprojT", S.val("oprojT"))
                        adst = attnb[:, :].rearrange("p (c t) -> p c t",
                                                     t=512)
                        for b in range(2):
                            sl = slice(b * 512, (b + 1) * 512)
                            S.inc(v.tensor_copy(rnbc[:, sl], p_R[:, sl]),
                                  "rnc", 1)
                            S.wait(v, "rnc")
                            inst = v.tensor_tensor(
                                adst[:, b * 4:(b + 1) * 4,
                                     jj * 128:(jj + 1) * 128],
                                p_C[:, sl].rearrange("p (c t) -> p c t",
                                                     t=128),
                                rnbc[:, sl].rearrange("p (c t) -> p c t",
                                                      t=128),
                                ALU.mult)
                            S.inc(inst, "aov", 1)

                mk["pendrn"] = rnchain3
                mk["pend3"] = tail3

            if mk.get("pendrn") is not None:
                mk["pendrn"]()
                mk["pendrn"] = None
            if mk.get("pend3") is not None:
                mk["pend3"]()
                mk["pend3"] = None

            for f in range(8 if p3sub >= 4 else 0):
                bs = (p_A, p_A, p_C, p_C)[f % 4][
                    :, (f % 2) * 512:(f % 2) * 512 + 512]

                @blk.sync
                def _(sync, f=f, tile=tile):
                    if tile == 0 and f == 0:
                        S.wait(sync, "p2done")
                    if f >= 2:
                        S.wait(sync, "resv", S.val("resv") - 1)
                    S.inc(sync.dma_start(
                        out=xres[f % 2][:, :],
                        in_=d_xsl[f * 128:(f + 1) * 128,
                                  tile * 512:(tile + 1) * 512]),
                        "xrl%d" % (f % 2), 16)

                @blk.tensor
                def _(t, f=f, bs=bs, tile=tile):
                    S.wait(t, "aov")
                    if f < 2:
                        S.wait(t, "maskv", S.val("maskv"))
                    elif f >= 4:
                        S.wait(t, "resv", mk[("resv", tile, f - 4)])
                    for c in range(8):
                        inst = t.matmul(bs,
                                        w_out[:, c * 1024 + f * 128:
                                              c * 1024 + (f + 1) * 128],
                                        attnb[:, c * 512:(c + 1) * 512],
                                        start=(c == 0), stop=(c == 7))
                    S.inc(inst, "oprojT", 1)

                @blk.vector
                def _(v, f=f, bs=bs):
                    S.wait(v, "oprojT")
                    S.wait(v, "xrl%d" % (f % 2))
                    i0 = v.scalar_tensor_tensor(af[:, f * 512:(f + 1) * 512],
                                                bs, c_outb[:, f:f + 1],
                                                xres[f % 2][:, :],
                                                ALU.add, ALU.add)
                    S.inc(i0, "resv", 1)
                    mk[("resv", tile, f)] = S.val("resv")

            for c in range(8 if p3sub >= 5 else 0):
                @blk.vector
                def _(v, c=c):
                    if c < 2:
                        S.wait(v, "resv")
                    if c >= 2:
                        S.wait(v, "statT", S.val("statT"))
                    S.inc(v.tensor_copy(ab[c % 2][:, :],
                                        af[:, c * 512:(c + 1) * 512]),
                          "abv", 1)

                @blk.gpsimd
                def _(g, c=c):
                    S.wait(g, "abv")
                    S.inc(g.tensor_mul(sqb[c % 2][:, :], ab[c % 2][:, :],
                                       ab[c % 2][:, :]), "sqp", 1)

                @blk.tensor
                def _(t, c=c):
                    S.wait(t, "abv")
                    if c == 0:
                        S.wait(t, "lnv", S.val("lnv"))
                    t.matmul(p_C[:, 0:512], c_ones[:, 0:128], ab[c % 2][:, :],
                             start=(c == 0), stop=(c == 7))
                    S.wait(t, "sqp")
                    inst = t.matmul(p_C[:, 512:1024], c_ones[:, 0:128],
                                    sqb[c % 2][:, :],
                                    start=(c == 0), stop=(c == 7))
                    S.inc(inst, "statT", 1)

            if p3sub < 5:
                continue

            @blk.vector
            def _(v):
                S.wait(v, "statT")
                S.inc(v.tensor_scalar_mul(msb[:, :], p_C[:, 0:512], 1.0 / E),
                      "lnv", 1)
                S.inc(v.tensor_scalar_mul(s2sb[:, :], p_C[:, 512:1024],
                                          1.0 / E), "lnv", 1)
                S.wait(v, "lnv")
                S.inc(v.tensor_mul(vsb[:, :], msb[:, :], msb[:, :]), "lnv", 1)
                S.wait(v, "lnv")
                S.inc(v.scalar_tensor_tensor(vsb[:, :], s2sb[:, :], 1e-5,
                                             vsb[:, :], ALU.add,
                                             ALU.subtract), "lnv", 1)
                S.wait(v, "lnv")

            @blk.scalar
            def _(sc):
                S.wait(sc, "lnv")
                S.inc(sc.activation(s2sb[:, :], vsb[:, :], AF.Sqrt,
                                    bias=0.0, scale=1.0), "lna", 1)

            @blk.vector
            def _(v):
                S.wait(v, "lna")
                S.inc(v.reciprocal(vsb[:, :], s2sb[:, :]), "lnv", 1)
                S.wait(v, "lnv")

            for c in range(8):
                @blk.vector
                def _(v, c=c, tile=tile):
                    S.wait(v, "lnv")
                    S.wait(v, "x1st0")
                    S.wait(v, "x1st1")
                    S.inc(v.tensor_sub(x1c[c % 2][:, :],
                                       af[:, c * 512:(c + 1) * 512],
                                       msb[:, :]), "lapp", 1)

                @blk.gpsimd
                def _(g, c=c):
                    S.wait(g, "lapp")
                    S.inc(g.tensor_mul(x1c[c % 2][:, :], x1c[c % 2][:, :],
                                       vsb[:, :]), "lappp", 1)

                @blk.vector
                def _(v, c=c, tile=tile):
                    S.wait(v, "lappp")
                    S.inc(v.tensor_copy(xlast[:, tile * 8 + c:
                                              tile * 8 + c + 1],
                                        x1c[c % 2][:, 511:512]), "x1v", 1)

                @blk.sync
                def _(sync, c=c, tile=tile):
                    S.wait(sync, "x1v")
                    S.inc(sync.dma_start(
                        out=d_x1[c, :, tile * 512:(tile + 1) * 512],
                        in_=x1c[c % 2][:, :]), "x1st%d" % (c % 2), 16)

        @blk.tensor
        def _(t):
            S.wait(t, "lnv", S.val("lnv"))
            S.wait(t, "lappp", S.val("lappp"))
            S.wait(t, "rn16v", S.val("rn16v"))
            S.wait(t, "rnbcv", S.val("rnbcv"))
            S.wait(t, "aov", S.val("aov"))
            S.wait(t, "maskv", S.val("maskv"))
            S.wait(t, "resv", S.val("resv"))
            S.inc(t.nop(), "p3done", 1)

        es3.close()
        es13.close()

        if phases < 4:
            raise _PhaseCut
        es4 = ExitStack()
        p_F = es4.enter_context(nc.psum_tensor("pF", [128, 1024], F32))
        p_G = es4.enter_context(nc.psum_tensor("pG", [128, 1024], F32))
        w_ffn = mksb(es4, "w_ffn", [128, 3 * 8 * 1024], BF16)
        x1g = [mksb(es4, "x1g0", [128, 8 * 512], F32),
               mksb(es4, "x1g1", [128, 8 * 512], F32)]
        x1p = [mksb(es4, "x1p0", [128, 8 * 512], F32),
               mksb(es4, "x1p1", [128, 8 * 512], F32)]
        gh = mksb(es4, "gh", [128, 8], F32)
        dtl = [mksb(es4, "dtl0", [128, 512], F32),
               mksb(es4, "dtl1", [128, 512], F32)]
        xrb = [mksb(es4, "xrb0", [128, 8 * 512], BF16),
               mksb(es4, "xrb1", [128, 8 * 512], BF16)]
        xkb = [mksb(es4, "xkb0", [128, 8 * 512], BF16),
               mksb(es4, "xkb1", [128, 8 * 512], BF16)]
        rb = mksb(es4, "rb", [128, 8 * 512], BF16)
        kab = mksb(es4, "kab", [128, 8 * 512], BF16)
        a2 = mksb(es4, "a2", [128, 8 * 512], F32)
        ab2r = [mksb(es4, "ab2r0", [128, 512], BF16),
                mksb(es4, "ab2r1", [128, 512], BF16)]
        sqr = [mksb(es4, "sqr0", [128, 512], BF16),
               mksb(es4, "sqr1", [128, 512], BF16)]
        ms2 = mksb(es4, "ms2", [128, 512], F32)
        s2s2 = mksb(es4, "s2s2", [128, 512], F32)
        vs2 = mksb(es4, "vs2", [128, 512], F32)

        @blk.gpsimd
        def _(g):
            S.wait(g, "p3done")
            for wi, dw in enumerate([d_wrt, d_wkt, d_wvt]):
                for c in range(8):
                    S.inc(g.dma_start(
                        out=w_ffn[:, wi * 8192 + c * 1024:
                                  wi * 8192 + (c + 1) * 1024],
                        in_=dw[c * 128:(c + 1) * 128, :]), "swf", 16)

        mk4 = dict(x1ld={}, mixv={}, mixp={}, rkT={}, a2v={}, yst={},
                   l2appp={}, ln2v={})

        def emit_ffn_head(tile):
            par = tile % 2

            @blk.sync
            def _(sync):
                S.wait(sync, "x1st0")
                S.wait(sync, "x1st1")
                if tile >= 2:
                    S.wait(sync, "a2v", mk4["a2v"][tile - 2])
                for c in range(8):
                    S.inc(sync.dma_start(
                        out=x1g[par][:, c * 512:(c + 1) * 512],
                        in_=d_x1[c, :, tile * 512:(tile + 1) * 512]),
                        "x1ld", 16)

            mk4["x1ld"][tile] = S.val("x1ld")

            @blk.vector
            def _(v):
                S.wait(v, "x1ld", mk4["x1ld"][tile])
                if tile >= 2:
                    S.wait(v, "yst", mk4["yst"][tile - 2])
                    S.wait(v, "rkT", mk4["rkT"][tile - 2])
                if tile == 0:
                    S.inc(v.tensor_copy(gh[:, :], x1m1[:, :]), "mixv", 1)
                else:
                    S.inc(v.tensor_copy(gh[:, :],
                                        xlast[:, (tile - 1) * 8:tile * 8]),
                          "mixv", 1)
                S.wait(v, "mixv")
                xg = x1g[par][:, :].rearrange("p (c t) -> p c t", t=512)
                xp = x1p[par][:, :].rearrange("p (c t) -> p c t", t=512)
                S.inc(v.tensor_copy(xp[:, :, 1:512], xg[:, :, 0:511]),
                      "mixv", 1)
                S.inc(v.tensor_copy(
                    xp[:, :, 0:1],
                    gh[:, :].rearrange("p (c o) -> p c o", o=1)), "mixv", 1)
                S.wait(v, "mixv")
                for c in range(8):
                    gi = tile * 8 + c
                    S.inc(v.tensor_sub(dtl[gi % 2][:, :],
                                       x1g[par][:, c * 512:(c + 1) * 512],
                                       x1p[par][:, c * 512:(c + 1) * 512]),
                          "mixv", 1)
                    S.wait(v, "mixv")
                    S.inc(v.scalar_tensor_tensor(
                        xrb[par][:, c * 512:(c + 1) * 512],
                        dtl[gi % 2][:, :], c_mur[:, c:c + 1],
                        x1p[par][:, c * 512:(c + 1) * 512],
                        ALU.mult, ALU.add), "mixv", 1)
                    S.inc(v.scalar_tensor_tensor(
                        xkb[par][:, c * 512:(c + 1) * 512],
                        dtl[gi % 2][:, :], c_muk[:, c:c + 1],
                        x1p[par][:, c * 512:(c + 1) * 512],
                        ALU.mult, ALU.add), "mixv", 1)

            mk4["mixv"][tile] = S.val("mixv")

        def emit_ffn_body(tile):
            par = tile % 2
            for f in range(8):
                bA = p_F[:, (f % 2) * 512:(f % 2) * 512 + 512]
                bB = p_G[:, (f % 2) * 512:(f % 2) * 512 + 512]

                @blk.tensor
                def _(t, f=f, bA=bA, bB=bB):
                    if f == 0:
                        S.wait(t, "mixv", mk4["mixv"][tile])
                        if tile == 0:
                            S.wait(t, "swf")
                        else:
                            S.wait(t, "a2v", mk4["a2v"][tile - 1])
                            S.wait(t, "ln2v", mk4["ln2v"][tile - 1])
                    if f >= 2:
                        S.wait(t, "sigm", mk4[("sig", tile, f - 2)])
                        S.wait(t, "kactv", mk4[("kact", tile, f - 2)])
                    for c in range(8):
                        t.matmul(bA,
                                 w_ffn[:, c * 1024 + f * 128:
                                       c * 1024 + (f + 1) * 128],
                                 xrb[par][:, c * 512:(c + 1) * 512],
                                 start=(c == 0), stop=(c == 7))
                    for c in range(8):
                        inst = t.matmul(
                            bB,
                            w_ffn[:, 8192 + c * 1024 + f * 128:
                                  8192 + c * 1024 + (f + 1) * 128],
                            xkb[par][:, c * 512:(c + 1) * 512],
                            start=(c == 0), stop=(c == 7))
                    S.inc(inst, "rkT", 1)

                @blk.scalar
                def _(sc, f=f, bA=bA):
                    S.wait(sc, "rkT")
                    S.inc(sc.activation(rb[:, f * 512:(f + 1) * 512], bA,
                                        AF.Sigmoid, bias=0.0, scale=1.0),
                          "sigm", 1)
                    mk4[("sig", tile, f)] = S.val("sigm")

                @blk.vector
                def _(v, f=f, bB=bB):
                    S.wait(v, "rkT")
                    S.inc(v.tensor_scalar_max(kab[:, f * 512:(f + 1) * 512],
                                              bB, 0.0), "kactv", 1)
                    S.wait(v, "kactv")
                    S.inc(v.tensor_mul(kab[:, f * 512:(f + 1) * 512],
                                       kab[:, f * 512:(f + 1) * 512],
                                       kab[:, f * 512:(f + 1) * 512]),
                          "kactv", 1)
                    mk4[("kact", tile, f)] = S.val("kactv")

            mk4["rkT"][tile] = S.val("rkT")
            sig_end = S.val("sigm")
            kact_end = S.val("kactv")

            for f in range(8):
                bA = p_F[:, (f % 2) * 512:(f % 2) * 512 + 512]

                @blk.tensor
                def _(t, f=f, bA=bA):
                    if f == 0:
                        S.wait(t, "sigm", sig_end)
                        S.wait(t, "kactv", kact_end)
                    if f >= 2:
                        S.wait(t, "a2v", mk4[("a2mul", tile, f - 2)])
                    for c in range(8):
                        inst = t.matmul(
                            bA,
                            w_ffn[:, 16384 + c * 1024 + f * 128:
                                  16384 + c * 1024 + (f + 1) * 128],
                            kab[:, c * 512:(c + 1) * 512],
                            start=(c == 0), stop=(c == 7))
                    S.inc(inst, "fvT", 1)

                if mk4.get("pendst") is not None:
                    mk4["pendst"]()
                    mk4["pendst"] = None

                @blk.vector
                def _(v, f=f, bA=bA):
                    S.wait(v, "fvT")
                    if S.val("st2T") >= 1:
                        S.wait(v, "st2T", S.val("st2T") - 1)
                    S.inc(v.tensor_tensor(a2[:, f * 512:(f + 1) * 512], bA,
                                          rb[:, f * 512:(f + 1) * 512],
                                          ALU.mult), "a2v", 1)
                    mk4[("a2mul", tile, f)] = S.val("a2v")
                    S.wait(v, "a2v")
                    S.inc(v.tensor_add(a2[:, f * 512:(f + 1) * 512],
                                       a2[:, f * 512:(f + 1) * 512],
                                       x1g[par][:, f * 512:(f + 1) * 512]),
                          "a2v", 1)
                    mk4[("a2add", tile, f)] = S.val("a2v")
                    S.wait(v, "a2v")
                    S.inc(v.tensor_copy(ab2r[f % 2][:, :],
                                        a2[:, f * 512:(f + 1) * 512]),
                          "a2v", 1)

                @blk.scalar
                def _(sc, f=f):
                    S.wait(sc, "a2v", mk4[("a2add", tile, f)])
                    if S.val("st2T") >= 1:
                        S.wait(sc, "st2T", S.val("st2T") - 1)
                    S.inc(sc.activation(sqr[f % 2][:, :],
                                        a2[:, f * 512:(f + 1) * 512],
                                        AF.Square, bias=0.0, scale=1.0),
                          "sqp2", 1)

                def st_blk(f=f):
                    @blk.tensor
                    def _(t):
                        S.wait(t, "a2v")
                        S.wait(t, "sqp2")
                        t.matmul(p_G[:, 0:512], c_ones[:, 0:128],
                                 ab2r[f % 2][:, :], start=(f == 0),
                                 stop=(f == 7), skip_group_check=True)
                        inst = t.matmul(p_G[:, 512:1024], c_ones[:, 0:128],
                                        sqr[f % 2][:, :], start=(f == 0),
                                        stop=(f == 7), skip_group_check=True)
                        S.inc(inst, "st2T", 1)

                mk4["pendst"] = st_blk

            if mk4.get("pendst") is not None:
                mk4["pendst"]()
                mk4["pendst"] = None

            mk4["a2v"][tile] = S.val("a2v")

            @blk.vector
            def _(v):
                S.wait(v, "st2T")
                if tile >= 1:
                    S.wait(v, "l2appp", mk4["l2appp"][tile - 1])
                S.inc(v.tensor_scalar_mul(ms2[:, :], p_G[:, 0:512], 1.0 / E),
                      "ln2v", 1)
                S.inc(v.tensor_scalar_mul(s2s2[:, :], p_G[:, 512:1024],
                                          1.0 / E), "ln2v", 1)
                S.wait(v, "ln2v")
                S.inc(v.tensor_mul(vs2[:, :], ms2[:, :], ms2[:, :]),
                      "ln2v", 1)
                S.wait(v, "ln2v")
                S.inc(v.scalar_tensor_tensor(vs2[:, :], s2s2[:, :], 1e-5,
                                             vs2[:, :], ALU.add,
                                             ALU.subtract), "ln2v", 1)
                S.wait(v, "ln2v")

            @blk.scalar
            def _(sc):
                S.wait(sc, "ln2v")
                S.inc(sc.activation(s2s2[:, :], vs2[:, :], AF.Sqrt,
                                    bias=0.0, scale=1.0), "ln2a", 1)

            @blk.vector
            def _(v):
                S.wait(v, "ln2a")
                S.inc(v.reciprocal(vs2[:, :], s2s2[:, :]), "ln2v", 1)

            mk4["ln2v"][tile] = S.val("ln2v")

            for c in range(8):
                @blk.vector
                def _(v, c=c):
                    S.wait(v, "ln2v")
                    S.inc(v.tensor_sub(x1p[par][:, c * 512:(c + 1) * 512],
                                       a2[:, c * 512:(c + 1) * 512],
                                       ms2[:, :]), "l2app", 1)

                @blk.gpsimd
                def _(g_, c=c):
                    S.wait(g_, "l2app")
                    S.inc(g_.tensor_mul(x1p[par][:, c * 512:(c + 1) * 512],
                                        x1p[par][:, c * 512:(c + 1) * 512],
                                        vs2[:, :]), "l2appp", 1)

            mk4["l2appp"][tile] = S.val("l2appp")

            @blk.sync
            def _(sync):
                S.wait(sync, "l2appp", mk4["l2appp"][tile])
                for c in range(8):
                    S.inc(sync.dma_start(
                        out=d_y[c, :, tile * 512:(tile + 1) * 512],
                        in_=x1p[par][:, c * 512:(c + 1) * 512]), "yst", 16)

            mk4["yst"][tile] = S.val("yst")

        emit_ffn_head(0)
        for tile in range(NTILE):
            if tile + 1 < NTILE:
                emit_ffn_head(tile + 1)
            emit_ffn_body(tile)

        es4.close()

        @blk.sync
        def _(sync):
            S.wait(sync, "yst")
    except _PhaseCutExc:
        pass

    return nc



def _prep_inputs(inputs, cfg: Cfg):
    x = np.ascontiguousarray(np.asarray(inputs["L0_attn_x"], np.float32)[0])
    xT = np.ascontiguousarray(x.T)
    T, PCH = cfg.T, cfg.PCH
    qkv_w = np.ascontiguousarray(np.asarray(inputs["L0_attn_qkv_w"], np.float32))
    out_w = np.ascontiguousarray(np.asarray(inputs["L0_attn_out_w"], np.float32))
    wrT = np.ascontiguousarray(np.asarray(inputs["L0_ffn_W_r"], np.float32).T)
    wkT = np.ascontiguousarray(np.asarray(inputs["L0_ffn_W_k"], np.float32).T)
    wvT = np.ascontiguousarray(np.asarray(inputs["L0_ffn_W_v"], np.float32).T)
    qkv_b = np.asarray(inputs["L0_attn_qkv_b"], np.float32)
    out_b = np.asarray(inputs["L0_attn_out_b"], np.float32)
    mu_r = np.asarray(inputs["L0_ffn_mu_r"], np.float32)
    mu_k = np.asarray(inputs["L0_ffn_mu_k"], np.float32)

    brow = np.ascontiguousarray(qkv_b[None, :])
    outb_fm = np.ascontiguousarray(out_b.reshape(8, 128).T)
    mur_fm = np.ascontiguousarray(mu_r.reshape(8, 128).T)
    muk_fm = np.ascontiguousarray(mu_k.reshape(8, 128).T)

    ident = np.eye(128, dtype=np.float32).astype(BB)
    maskc = np.tile(np.triu(np.ones((CH, CH), np.float32)), (1, 4)).astype(BB)
    e31 = np.zeros((128, 31), np.float32)
    e31[:, 15] = 1.0
    e31 = e31.astype(BB)
    ebc = np.zeros((16, 8 * 128), np.float32)
    for c in range(8):
        for m in range(128):
            ebc[2 * c + m // 64, c * 128 + m] = 1.0
    ebc = ebc.astype(BB)
    ones = np.ones((128, 512), np.float32).astype(BB)

    in_maps = []
    for i in range(cfg.ncores):
        start = i * T
        xpfx = np.zeros((E, PCH * CH), np.float32)
        nreal = min(start, PCH * CH)
        if nreal > 0:
            xpfx[:, PCH * CH - nreal:] = xT[:, start - nreal:start]
        xsl = np.ascontiguousarray(xT[:, start:start + T])
        if start > 0:
            xm1 = np.ascontiguousarray(xT[:, start - 1].reshape(8, 128).T)
        else:
            xm1 = np.zeros((128, 8), np.float32)
        pmask = np.zeros((128, PCH), np.float32)
        nrealch = nreal // CH
        if nrealch > 0:
            pmask[:, PCH - nrealch:] = 1.0
        m0 = np.full((128, 1), 0.0 if i == 0 else 1.0, np.float32)
        in_maps.append(dict(
            xpfx=xpfx, xsl=xsl, xm1=xm1, pmask=pmask, m0=m0,
            qkv_w=qkv_w, out_w=out_w, wrT=wrT, wkT=wkT, wvT=wvT,
            brow=brow, outb_fm=outb_fm, mur_fm=mur_fm, muk_fm=muk_fm,
            ident=ident, maskc=maskc, e31=e31, ebc=ebc, ones=ones,
        ))
    return in_maps


def kernel(**inputs):
    cfg = Cfg()
    in_maps = _prep_inputs(inputs, cfg)
    nc = build_nc(cfg)
    res = run_bass_kernel_spmd(nc, in_maps, core_ids=list(range(cfg.ncores)))
    outs = []
    for i in range(cfg.ncores):
        y = np.asarray(res.results[i]["y"])
        outs.append(y.transpose(2, 0, 1).reshape(cfg.T, E))
    return np.concatenate(outs, axis=0)[None].astype(np.float32)
